# version 09 (final)
# baseline (speedup 1.0000x reference)
import contextlib
import numpy as np
import concourse.bass as bass
import concourse.mybir as mybir
from concourse.bass_utils import run_bass_kernel_spmd
from concourse.aot_env import allow_ici_all_to_all

F32 = mybir.dt.float32
F32R = mybir.dt.float32r
BF16 = mybir.dt.bfloat16
U8 = mybir.dt.uint8
AF = mybir.ActivationFunctionType
ALU = mybir.AluOpType

NCORE = 8
DM, NST, RDT = 4096, 16, 128
TPC = 1024
DSH = 512
NDT = DSH // 128
LB = 4096
TH = 2048
Q0, Q1 = 611, 4095
EPS = 1e-5

DEBUG_TAPS = ()
MAX_PHASE = 99
SERIALIZE = 0
SER_ENGINES = ("sync", "gpsimd", "vector", "scalar", "tensor")
SER_EXEMPT = ()


class Plan:
    def __init__(self):
        self.items = []
        self.val = {}
        self.ser_sem = None
        self.scope = None

    def emit(self, engine, fn, wait=(), inc=(), ser=True, tag=""):
        if SERIALIZE and ser and engine in SER_ENGINES and tag not in SER_EXEMPT:
            wait = tuple(wait) + tuple((sm, v) for sm, v in self.val.items())
        for s, a in inc:
            self.val[s] = self.val.get(s, 0) + a
        self.items.append((engine, fn, tuple(wait), tuple(inc), self.scope))

    def cur(self, sem):
        return self.val.get(sem, 0)

    def replay(self, engine_name, eng, nc=None):
        import contextlib as _ctx
        cur_scope = None
        guard = None
        for item in self.items:
            e, fn, waits, incs = item[:4]
            scope = item[4] if len(item) > 4 else None
            if e != engine_name:
                continue
            if nc is not None and scope != cur_scope:
                if guard is not None:
                    guard.__exit__(None, None, None)
                    guard = None
                if scope is not None:
                    guard = nc.named_scope(scope)
                    guard.__enter__()
                cur_scope = scope
            for s, v in waits:
                if v > 0:
                    eng.wait_ge(s, v)
            inst = fn(eng)
            for s, a in incs:
                inst = inst.then_inc(s, a)
        if guard is not None:
            guard.__exit__(None, None, None)


class SbufPool:
    def __init__(self, nc, cx, name, nbytes):
        self.t = cx.enter_context(nc.sbuf_tensor(name, [128, nbytes], U8))
        self.n = nbytes
        self.off = 0
        self.marks = []
        self.hi = 0

    def take(self, cols, dt, parts=128, name=""):
        esz = mybir.dt.size(dt)
        off = (self.off + 3) // 4 * 4
        nb = cols * esz
        assert off + nb <= self.n, f"SbufPool overflow at {name}: {(off+nb)//1024}KB > {self.n//1024}KB"
        self.off = off + nb
        self.hi = max(self.hi, self.off)
        return self.t.ap()[0:parts, off:off + nb].bitcast(dt)

    def push(self):
        self.marks.append(self.off)

    def pop(self):
        self.off = self.marks.pop()


def build_program():
    nc = bass.Bass("TRN2", target_bir_lowering=False)
    dram = lambda n, sh, dt=F32, **kw: nc.dram_tensor(n, sh, dt, **kw)

    x_own = dram("x_own", [TPC, DM], kind="ExternalInput").ap()
    xpwT = dram("xpwT", [DM, 160], kind="ExternalInput").ap()
    dtpwT = dram("dtpwT", [RDT, DSH], kind="ExternalInput").ap()
    dtpb = dram("dtpb", [128, NDT], kind="ExternalInput").ap()
    alog = dram("alog", [128, NDT * NST], kind="ExternalInput").ap()
    dvecs = dram("dvecs", [128, NDT], kind="ExternalInput").ap()
    w_T = dram("w_T", [DM, DM], kind="ExternalInput").ap()
    ffnb = dram("ffnb", [128, 32], kind="ExternalInput").ap()
    aallT = dram("aallT", [DM, 128], kind="ExternalInput").ap()
    bstkT = dram("bstkT", [128, DM], kind="ExternalInput").ap()
    rwT = dram("rwT", [DM, 8], kind="ExternalInput").ap()
    rbias = dram("rbias", [8, 1], kind="ExternalInput").ap()
    identf = dram("identf", [128, 128], kind="ExternalInput").ap()
    qmask = dram("qmask", [128, 8], kind="ExternalInput").ap()
    dvrow = dram("dvrow", [1, DM], kind="ExternalInput").ap()
    out2_h = dram("out2", [TPC, DM], kind="ExternalOutput").ap()

    a2a1_in = dram("a2a1_in", [NCORE, DSH, TPC], BF16)
    a2a1_out = dram("a2a1_out", [NCORE, DSH, TPC], BF16)
    ag1_in = dram("ag1_in", [160, TPC], BF16)
    ag1_out = dram("ag1_out", [NCORE * 160, TPC], BF16)
    a2a2_in = dram("a2a2_in", [NCORE, TPC, DSH])
    a2a2_out = dram("a2a2_out", [NCORE, TPC, DSH])
    ar3_in = dram("ar3_in", [128, 32])
    ar3_out = dram("ar3_out", [128, 32])
    rt_bounce = dram("rt_bounce", [1, 8])
    lg_bounce = dram("lg_bounce", [8, 1])
    out1_scr = dram("out1_scr", [TPC, DM])
    ffn_scr = dram("ffn_scr", [TPC, DM])

    taps = {}
    def tap(name, sh, dt=F32):
        if name in DEBUG_TAPS:
            taps[name] = dram("tap_" + name, sh, dt, kind="ExternalOutput").ap()
        return taps.get(name)

    tap_xp = tap("xp", [160, TPC])
    tap_xlnT = tap("xlnT", [NCORE, DSH, TPC], BF16)
    tap_dtT = tap("dtT", [DSH, 2 * LB])
    tap_u = tap("u", [DSH, 2 * LB])
    tap_ym = tap("ym", [TPC, DM])
    tap_yslab = tap("yslab", [NCORE, TPC, DSH])
    tap_ab1 = tap("ab1", [128, TH])
    tap_vb1 = tap("vb1", [128, TH])
    tap_hb1 = tap("hb1", [128, TH])
    tap_gb1 = tap("gb1", [128, TH])
    tap_yst1 = tap("yst1", [128, TH])
    tap_hl = tap("hl", [128, NST])
    tap_a2 = tap("a2", [128, TH])
    tap_v2 = tap("v2", [128, TH])
    tap_h2 = tap("h2", [128, TH])
    tap_out1 = tap("out1", [TPC, DM])
    tap_xln3 = tap("xln3", [TPC, DM])
    tap_rt = tap("rt", [1, 8])
    tap_lgx = tap("lgx", [1, 8])
    tap_qsf = tap("qsf", [128, 32])
    tap_qsb = tap("qsb", [128, 32])
    tap_t1s = tap("t1s", [128, TPC])

    cx = contextlib.ExitStack()
    with cx:
        sem = lambda n: cx.enter_context(nc.semaphore(n))
        s_ld = sem("s_ld"); s_gld = sem("s_gld"); s_vec = sem("s_vec")
        s_act = sem("s_act"); s_pe = sem("s_pe"); s_cc = sem("s_cc")
        s_st = sem("s_st"); s_ev = sem("s_ev")
        s_pl = sem("s_pl")
        s_gc = sem("s_gc")
        s_dt = sem("s_dt")
        s_xt = [sem("s_xt0"), sem("s_xt1")]
        s_x = [sem("s_x0"), sem("s_x1")]
        s_bb = [sem("s_bb0"), sem("s_bb1")]
        s_ym = [sem("s_ym0"), sem("s_ym1")]
        s_xr = [sem("s_xr0"), sem("s_xr1")]
        s_p4 = [sem("s_p40"), sem("s_p41")]
        s_w = [sem("s_w0"), sem("s_w1")]
        P_sersem = sem("s_ser")

        pool = SbufPool(nc, cx, "pool", 195 * 1024)
        sb_r = lambda n, sh: cx.enter_context(nc.sbuf_tensor(n, sh, F32R))
        gbuf_t = sb_r("gbuf_t", [128, TH])
        fsb_t = sb_r("fsb_t", [128, TPC])
        identr_t = sb_r("identr_t", [128, 128])
        bank = [cx.enter_context(nc.psum_tensor(f"bank{i}", [128, 512], F32)) for i in range(8)]

        P = Plan()
        P.ser_sem = P_sersem

        ident_b = pool.take(128, BF16, name="ident_b")
        ident_r = identr_t.ap()
        dtpb_s = pool.take(NDT, F32, name="dtpb_s")
        dvec_s = pool.take(NDT, F32, name="dvec_s")
        alog_s = pool.take(NDT * NST, F32, name="alog_s")
        adev = pool.take(NDT * NST, F32, name="adev")
        qmask_b = pool.take(8, BF16, name="qmask_b")
        rbias_s = pool.take(1, F32, parts=8, name="rbias_s")
        ffnb_s = pool.take(32, F32, name="ffnb_s")
        dtpwT_b = pool.take(DSH, BF16, parts=RDT, name="dtpwT_b")
        cst = pool.take(4, F32, name="cst")

        def init_cst(e):
            e.memset(cst[:, 0:1], float(EPS))
            e.memset(cst[:, 1:2], -0.5)
            return e.memset(cst[:, 2:3], -1.0)
        P.emit("vector", init_cst, inc=[(s_vec, 1)])
        cst_done = P.cur(s_vec)
        P.emit("sync", lambda e: e.dma_start(out=dtpb_s, in_=dtpb[:, :]), inc=[(s_ld, 16)])
        P.emit("sync", lambda e: e.dma_start(out=dvec_s, in_=dvecs[:, :]), inc=[(s_ld, 16)])
        P.emit("sync", lambda e: e.dma_start(out=alog_s, in_=alog[:, :]), inc=[(s_ld, 16)])
        P.emit("sync", lambda e: e.dma_start(out=rbias_s, in_=rbias[:, :]), inc=[(s_ld, 16)])
        P.emit("sync", lambda e: e.dma_start(out=ffnb_s, in_=ffnb[:, :]), inc=[(s_ld, 16)])

        P.emit("gpsimd", lambda e: e.dma_start(out=ident_b, in_=identf[:, :]), inc=[(s_gc, 16)])
        P.emit("gpsimd", lambda e: e.dma_start(out=ident_r, in_=identf[:, :]), inc=[(s_gc, 16)])
        P.emit("gpsimd", lambda e: e.dma_start(out=qmask_b, in_=qmask[:, :]), inc=[(s_gc, 16)])
        P.emit("gpsimd", lambda e: e.dma_start(out=dtpwT_b, in_=dtpwT[:, :]), inc=[(s_gc, 16)])
        P.emit("gpsimd", lambda e: e.dma_start(out=xpwT_b.rearrange("p (c r) -> p c r", r=160), in_=xpwT.rearrange("(c p) r -> p c r", p=128)), inc=[(s_gc, 16)])
        ld_const = P.cur(s_ld)
        gld_const = P.cur(s_gc)
        P.emit("scalar", lambda e: e.activation(out=adev, in_=alog_s, func=AF.Exp),
               wait=[(s_ld, ld_const), (s_vec, cst_done)], inc=[(s_act, 1)])
        P.emit("vector", lambda e: e.tensor_scalar_mul(adev, adev, -1.0),
               wait=[(s_act, P.cur(s_act))], inc=[(s_vec, 1)])
        adev_rdy = P.cur(s_vec)

        P.scope = "p0"
        pool.push()
        xpwT_b = pool.take(32 * 160, BF16, name="xpwT_b")
        xbuf = [pool.take(DM, F32, name="xbuf") for _ in range(2)]
        st48 = pool.take(48, F32, name="st48")
        mv01 = pool.take(2, F32, name="mv01")
        rstd0 = pool.take(2, F32, name="rstd0")
        xln_b = [pool.take(DM, BF16, name="xln_b") for _ in range(2)]
        xlnT_t = [pool.take(DM, BF16, name="xlnT_t") for _ in range(2)]
        xpst = pool.take(8 * 128, BF16, name="xpst")
        bcst = pool.take(8 * 128, BF16, parts=32, name="bcst")

        pT = [bank[6].ap().bitcast(BF16), bank[7].ap().bitcast(BF16)]
        pXP1 = bank[2].ap()[:, 0:128]
        pXP2 = bank[3].ap()[0:32, 0:128]

        v_norm = [0, 0]
        pe_xuse = [0, 0]
        ev_xlnT = [0, 0]
        for tt in range(8):
            bi = tt % 2
            w = [(s_vec, v_norm[bi])] if tt >= 2 else []
            P.emit("sync", lambda e, tt=tt, bi=bi: e.dma_start(out=xbuf[bi], in_=x_own[tt * 128:(tt + 1) * 128, :]),
                   wait=w, inc=[(s_x[bi], 16)])
            ldx = P.cur(s_x[bi])

            def stats_a(e, bi=bi):
                last = None
                for c in range(8):
                    last = e.bn_stats(st48[:, c * 6:(c + 1) * 6], xbuf[bi][:, c * 512:(c + 1) * 512])
                return last
            P.emit("vector", stats_a, wait=[(s_x[bi], ldx)], inc=[(s_vec, 1)], tag="v0")
            P.emit("vector", lambda e: e.bn_aggr(mv01, st48), wait=[(s_vec, P.cur(s_vec))], inc=[(s_vec, 1)], tag="v0")
            v_st = P.cur(s_vec)

            P.emit("scalar", lambda e: e.activation(out=rstd0[:, 0:1], in_=mv01[:, 1:2], func=AF.Ln, bias=cst[:, 0:1]),
                   wait=[(s_vec, v_st)], inc=[(s_act, 1)])
            P.emit("scalar", lambda e: e.activation(out=rstd0[:, 1:2], in_=rstd0[:, 0:1], func=AF.Exp, scale=cst[:, 1:2]),
                   wait=[(s_act, P.cur(s_act))], inc=[(s_act, 1)])

            w = [(s_act, P.cur(s_act)), (s_pe, pe_xuse[bi]), (s_vec, v_st)]
            P.emit("vector", lambda e, bi=bi: e.tensor_scalar(xln_b[bi], xbuf[bi], mv01[:, 0:1], rstd0[:, 1:2], ALU.subtract, ALU.mult),
                   wait=w, inc=[(s_vec, 1)], tag="v0")
            v_norm[bi] = P.cur(s_vec)

            for g4 in range(8):
                pbi = g4 % 2
                def transp(e, bi=bi, g4=g4, pbi=pbi):
                    last = None
                    for c4 in range(4):
                        c = g4 * 4 + c4
                        last = e.matmul(pT[pbi][:, c4 * 128:(c4 + 1) * 128],
                                        xln_b[bi][:, c * 128:(c + 1) * 128],
                                        ident_b, is_transpose=True, start=True, stop=True)
                    return last
                w = [(s_vec, v_norm[bi]), (s_gc, gld_const)]
                if tt > 0 or g4 >= 2:
                    w.append((s_act, act_ev[pbi]))
                P.emit("tensor", transp, wait=w, inc=[(s_pe, 1)])
                if g4 == 7:
                    pe_xuse[bi] = P.cur(s_pe)
                pe_t = P.cur(s_pe)
                if tt == 0 and g4 == 0:
                    act_ev = [0, 0]
                w = [(s_pe, pe_t)]
                if tt >= 2 and g4 < 2:
                    w.append((s_ev, ev_xlnT[bi]))
                P.emit("scalar", lambda e, bi=bi, g4=g4, pbi=pbi: e.copy(
                        out=xlnT_t[bi][:, g4 * 512:(g4 + 1) * 512], in_=pT[pbi][:, 0:512]),
                       wait=w, inc=[(s_act, 1)])
                act_ev[pbi] = P.cur(s_act)
            a_tile = P.cur(s_act)
            P.emit("gpsimd", lambda e, bi=bi, tt=tt: e.dma_start(
                    out=a2a1_in.ap()[:, :, tt * 128:(tt + 1) * 128].rearrange("g (q p) t -> p (g q) t", p=128),
                    in_=xlnT_t[bi].rearrange("p (c t) -> p c t", c=32)),
                   wait=[(s_act, a_tile)], inc=[(s_gld, 16)])
            g_slab = P.cur(s_gld)
            def xpmm(e, bi=bi):
                w3 = xpwT_b.rearrange("p (c r) -> p c r", r=160)
                last = None
                for c in range(32):
                    mv_ap = xlnT_t[bi][:, c * 128:(c + 1) * 128]
                    e.matmul(pXP1, w3[:, c, 0:128], mv_ap, start=(c == 0), stop=(c == 31))
                    last = e.matmul(pXP2, w3[:, c, 128:160], mv_ap, start=(c == 0), stop=(c == 31))
                return last
            P.emit("tensor", xpmm, wait=[(s_act, a_tile)], inc=[(s_pe, 1)])
            pe_xp = P.cur(s_pe)
            P.emit("scalar", lambda e, tt=tt: (e.copy(out=xpst[:, tt * 128:(tt + 1) * 128], in_=pXP1),
                                               e.copy(out=bcst[:, tt * 128:(tt + 1) * 128], in_=pXP2))[-1],
                   wait=[(s_pe, pe_xp)], inc=[(s_act, 1)])
            P.emit("gpsimd", lambda e: e.engine_nop(), wait=[(s_gld, g_slab), (s_pe, pe_xp)], inc=[(s_ev, 1)], ser=False)
            ev_xlnT[bi] = P.cur(s_ev)

        a_xp = P.cur(s_act)
        P.emit("sync", lambda e: e.dma_start(out=ag1_in[0:128, :], in_=xpst), wait=[(s_act, a_xp)], inc=[(s_st, 16)])
        P.emit("sync", lambda e: e.dma_start(out=ag1_in[128:160, :], in_=bcst), inc=[(s_st, 16)])
        st_xp = P.cur(s_st)
        gld_slabs = P.cur(s_gld)

        P.scope = "x1"
        P.emit("gpsimd", lambda e: e.collective_compute(
                "AllGather", ALU.bypass, replica_groups=[list(range(NCORE))],
                ins=[ag1_in[:, :]], outs=[ag1_out[:, :]]),
               wait=[(s_st, st_xp), (s_gld, gld_slabs)], inc=[(s_cc, 1)], ser=False)
        def a2a1(e):
            with allow_ici_all_to_all():
                return e.collective_compute(
                    "AllToAll", ALU.bypass, replica_groups=[list(range(NCORE))],
                    ins=[a2a1_in.ap().rearrange("g d t -> (g d) t")],
                    outs=[a2a1_out.ap().rearrange("g d t -> (g d) t")])
        P.emit("gpsimd", a2a1, inc=[(s_cc, 1)], ser=False)
        cc_x1 = P.cur(s_cc)
        if tap_xp is not None:
            P.emit("sync", lambda e: e.dma_start(out=tap_xp[:, :], in_=ag1_in[:, :]),
                   wait=[(s_st, st_xp)], inc=[(s_st, 16)])
        if tap_xlnT is not None:
            P.emit("gpsimd", lambda e: e.dma_start(out=tap_xlnT[:, :, :], in_=a2a1_out[:, :, :]),
                   wait=[(s_cc, cc_x1)], inc=[(s_gld, 16)])
        pool.pop()

        env = dict(locals())
        if MAX_PHASE >= 1:
            emit_phase1(nc, P, pool, bank, env)
        if MAX_PHASE >= 2:
            emit_phase23(nc, P, pool, bank, env)

        P.emit("gpsimd", lambda e: e.engine_nop(), wait=[(s_st, P.cur(s_st)), (s_gld, P.cur(s_gld))], ser=False)

        with nc.Block() as block:
            @block.sync
            def _(eng):
                P.replay("sync", eng, nc)
            @block.gpsimd
            def _(eng):
                P.replay("gpsimd", eng, nc)
            @block.vector
            def _(eng):
                P.replay("vector", eng, nc)
            @block.scalar
            def _(eng):
                P.replay("scalar", eng, nc)
            @block.tensor
            def _(eng):
                P.replay("tensor", eng, nc)

    return nc, taps


def emit_phase1(nc, P, pool, bank, env):
    g = env
    s_ld, s_gld, s_vec, s_act, s_pe, s_cc, s_st, s_ev = (
        g["s_ld"], g["s_gld"], g["s_vec"], g["s_act"], g["s_pe"], g["s_cc"], g["s_st"], g["s_ev"])
    s_gc, s_dt, s_xt, s_bb = g["s_gc"], g["s_dt"], g["s_xt"], g["s_bb"]
    s_pl = g["s_pl"]
    ag1_out, a2a1_out, a2a2_in, a2a2_out = g["ag1_out"], g["a2a1_out"], g["a2a2_in"], g["a2a2_out"]
    dtpwT_b, dtpb_s, adev, dvec_s, ident_r = g["dtpwT_b"], g["dtpb_s"], g["adev"], g["dvec_s"], g["ident_r"]
    ident_b = g["ident_b"]
    cst = g["cst"]
    cc_x1 = g["cc_x1"]
    adev_rdy = g["adev_rdy"]
    gb = g["gbuf_t"].ap()
    gbufB = [gb[:, 0:TH // 2].bitcast(BF16), gb[:, TH // 2:TH].bitcast(BF16)]

    P.scope = "p1"
    pool.push()
    dtT = [pool.take(2 * LB, BF16, name="dtT") for _ in range(NDT)]
    u_b = [pool.take(2 * LB, BF16, name="u_b") for _ in range(NDT)]
    abuf = [pool.take(TH, F32, name="abuf") for _ in range(2)]
    hlast = pool.take(NDT * NST, F32, name="hlast")
    Bb = [pool.take(TH, BF16, name="Bb") for _ in range(2)]
    Cb = [pool.take(TH, BF16, name="Cb") for _ in range(2)]
    pool.push()
    dtinT_b = pool.take(2 * LB, BF16, parts=RDT, name="dtinT_b")
    sp1 = [pool.take(512, BF16, name="sp1") for _ in range(2)]
    sp2 = [pool.take(512, BF16, name="sp2") for _ in range(2)]
    pool.pop()
    vbuf = [pool.take(TH, BF16, name="vbuf") for _ in range(2)]
    hbuf = [pool.take(TH, BF16, name="hbuf") for _ in range(2)]
    yst = pool.take(TH // 2, F32, name="yst")
    xlnT_s = u_b

    for j in range(NCORE):
        P.emit("gpsimd", lambda e, j=j: e.dma_start(out=dtinT_b[:, j * TPC:(j + 1) * TPC],
                                                    in_=ag1_out[160 * j:160 * j + 128, :]),
               wait=[(s_cc, cc_x1 - 1)] if j == 0 else [], inc=[(s_dt, 16)])
    g_dtin = P.cur(s_dt)

    pDT = [bank[4].ap(), bank[5].ap()]
    st = dict(act_sp=[0, 0], v_dt=[0, 0], v_u=0)

    for dt_i in range(NDT):
        xb = dt_i % 2
        for j in range(NCORE):
            w = []
            if dt_i == 0 and j == 0:
                w = [(s_cc, cc_x1)]
            if dt_i >= 2 and j == 0:
                w = [(s_vec, st[f"vu{dt_i - 2}"])]
            P.emit("gpsimd", lambda e, j=j, dt_i=dt_i: e.dma_start(
                    out=u_b[dt_i][:, j * TPC:(j + 1) * TPC],
                    in_=a2a1_out[j, dt_i * 128:(dt_i + 1) * 128, :]),
                   wait=w, inc=[(s_xt[xb], 16)])
        g_xln = P.cur(s_xt[xb])
        for ch in range(16):
            pb = ch % 2
            w = [(s_dt, g_dtin)]
            if st["act_sp"][pb]:
                w.append((s_act, st["act_sp"][pb]))
            P.emit("tensor", lambda e, dt_i=dt_i, ch=ch, pb=pb: e.matmul(
                    pDT[pb], dtpwT_b[:, dt_i * 128:(dt_i + 1) * 128],
                    dtinT_b[:, ch * 512:(ch + 1) * 512], start=True, stop=True),
                   wait=w, inc=[(s_pe, 1)])
            pe_mm = P.cur(s_pe)
            bias_ap = dtpb_s[:, dt_i:dt_i + 1]
            w = [(s_pe, pe_mm)]
            if st["v_dt"][pb]:
                w.append((s_vec, st["v_dt"][pb]))
            P.emit("scalar", lambda e, pb=pb, bias_ap=bias_ap: e.activation(out=sp1[pb], in_=pDT[pb], func=AF.Abs, bias=bias_ap),
                   wait=w, inc=[(s_act, 1)])
            P.emit("scalar", lambda e, pb=pb: e.activation(out=sp2[pb], in_=sp1[pb], func=AF.Exp, scale=cst[:, 2:3]),
                   wait=[(s_act, P.cur(s_act))], inc=[(s_act, 1)])
            P.emit("scalar", lambda e, pb=pb: e.activation(out=sp1[pb], in_=sp2[pb], func=AF.Ln, bias=1.0),
                   wait=[(s_act, P.cur(s_act))], inc=[(s_act, 1)])
            P.emit("scalar", lambda e, pb=pb, bias_ap=bias_ap: e.activation(out=sp2[pb], in_=pDT[pb], func=AF.Relu, bias=bias_ap),
                   wait=[(s_act, P.cur(s_act))], inc=[(s_act, 1)])
            st["act_sp"][pb] = P.cur(s_act)
            P.emit("vector", lambda e, dt_i=dt_i, ch=ch, pb=pb: e.tensor_add(dtT[dt_i][:, ch * 512:(ch + 1) * 512], sp1[pb], sp2[pb]),
                   wait=[(s_act, st["act_sp"][pb])], inc=[(s_vec, 1)], tag="vadd")
            st["v_dt"][pb] = P.cur(s_vec)
        v_dtall = P.cur(s_vec)
        P.emit("vector", lambda e, dt_i=dt_i: e.tensor_mul(u_b[dt_i], dtT[dt_i], u_b[dt_i]),
               wait=[(s_vec, v_dtall), (s_xt[xb], g_xln)], inc=[(s_vec, 1)], tag="vu")
        st[f"vu{dt_i}"] = P.cur(s_vec)
        st["v_u"] = P.cur(s_vec)
    v_prep = P.cur(s_vec)
    a_prep = P.cur(s_act)

    stt = dict(a_free=[0, 0], v_use=[0, 0], pe_gfree=[0, 0], bc_free=[0, 0],
               ev_y_pk=[0, 0])
    pend = []
    evacs = []

    def make_evac(j0, dp, pe_y, st0):
        def flush():
            for idx, dt_i in enumerate((2 * dp, 2 * dp + 1)):
                pk = dt_i % 2
                for half in range(2):
                    prev = st0 + 16 * (2 * idx + half)
                    def evy(e, pk=pk, half=half):
                        e.copy(out=yst[:, 0:512], in_=bank[4 * pk + 2 * half].ap())
                        return e.copy(out=yst[:, 512:1024], in_=bank[4 * pk + 2 * half + 1].ap())
                    w = [(s_pe, pe_y)]
                    if prev > 0:
                        w.append((s_st, prev))
                    P.emit("scalar", evy, wait=w, inc=[(s_act, 1)])
                    a_ev = P.cur(s_act)
                    P.emit("sync", lambda e, dt_i=dt_i, j0=j0, half=half: e.dma_start(
                            out=a2a2_in[j0 + half, :, dt_i * 128:(dt_i + 1) * 128]
                                .rearrange("(bq p) c -> p bq c", p=128),
                            in_=yst.rearrange("p (bq c) -> p bq c", c=128)),
                           wait=[(s_act, a_ev)], inc=[(s_st, 16)])
        return flush

    def emit_g(pk, nb, n, scan_v):
        w = [(s_vec, P.cur(s_vec))]
        if stt["pe_gfree"][pk]:
            w.append((s_pe, stt["pe_gfree"][pk]))
        P.emit("vector", lambda e, pk=pk, nb=nb: e.tensor_mul(gbufB[pk], hbuf[pk], Cb[nb]),
               wait=w, inc=[(s_vec, 1)], tag="vg")
        g_g = P.cur(s_vec)
        if pk == 1:
            stt["bc_free"][nb] = g_g
        def tacc(e, n=n, pk=pk):
            last = None
            for tb in range(16):
                bk = 4 * pk + tb // 4
                col = (tb % 4) * 128
                last = e.matmul(bank[bk].ap()[:, col:col + 128],
                                gbufB[pk][:, tb * 128:(tb + 1) * 128],
                                ident_b,
                                start=(n == 0 and tb % 4 == 0),
                                stop=(n == NST - 1 and tb % 4 == 3))
            return last
        w = [(s_vec, g_g)]
        if n == 0 and stt["ev_y_pk"][pk]:
            w.append((s_st, stt["ev_y_pk"][pk]))
        P.emit("tensor", tacc, wait=w, inc=[(s_pe, 1)])
        stt["pe_gfree"][pk] = P.cur(s_pe)

    it = 0
    for b in range(2):
        for th in range(2):
            toff = b * LB + th * TH
            j0 = toff // TPC
            for dp in range(2):
                for n in range(NST):
                    nb = it % 2
                    if n == 1 and evacs:
                        evacs.pop(0)()
                    def bcb(e, jq, b, th, n, nb, row0, dst):
                        j = 4 * b + th * 2 + jq
                        src = ag1_out.ap()[160 * j + row0 + n:160 * j + row0 + n + 1, :].partition_broadcast(128)
                        return e.dma_start(out=dst[nb][:, jq * TPC:(jq + 1) * TPC],
                                           in_=src.rearrange("p o t -> p (o t)"))
                    w = [(s_vec, stt["bc_free"][nb])] if stt["bc_free"][nb] else []
                    P.emit("gpsimd", lambda e, b=b, th=th, n=n, nb=nb: bcb(e, 0, b, th, n, nb, 128, Bb), wait=w, inc=[(s_bb[nb], 16)], ser=False)
                    P.emit("gpsimd", lambda e, b=b, th=th, n=n, nb=nb: bcb(e, 1, b, th, n, nb, 128, Bb), inc=[(s_bb[nb], 16)], ser=False)
                    P.emit("gpsimd", lambda e, b=b, th=th, n=n, nb=nb: bcb(e, 0, b, th, n, nb, 144, Cb), inc=[(s_bb[nb], 16)], ser=False)
                    P.emit("gpsimd", lambda e, b=b, th=th, n=n, nb=nb: bcb(e, 1, b, th, n, nb, 144, Cb), inc=[(s_bb[nb], 16)], ser=False)
                    g_bc = P.cur(s_bb[nb])
                    for dt_i in (2 * dp, 2 * dp + 1):
                        pk = dt_i % 2
                        w = [(s_vec, adev_rdy), (s_vec, v_prep)]
                        if stt["a_free"][pk]:
                            w.append((s_vec, stt["a_free"][pk]))
                        P.emit("scalar", lambda e, dt_i=dt_i, n=n, pk=pk, toff=toff: e.activation(
                                out=abuf[pk], in_=dtT[dt_i][:, toff:toff + TH], func=AF.Exp,
                                scale=adev[:, dt_i * NST + n:dt_i * NST + n + 1]),
                               wait=w, inc=[(s_act, 1)])
                        a_rdy = P.cur(s_act)
                        P.emit("vector", lambda e, dt_i=dt_i, nb=nb, toff=toff, pk=pk: e.tensor_mul(
                                vbuf[pk], u_b[dt_i][:, toff:toff + TH], Bb[nb]),
                               wait=[(s_bb[nb], g_bc), (s_vec, P.cur(s_vec))], inc=[(s_vec, 1)], tag="vv")
                        if pend:
                            emit_g(*pend.pop(0))
                        hl = hlast[:, dt_i * NST + n:dt_i * NST + n + 1]
                        w = [(s_act, a_rdy), (s_vec, P.cur(s_vec))]
                        if th == 0:
                            P.emit("vector", lambda e, pk=pk: e.tensor_tensor_scan(
                                    hbuf[pk], abuf[pk], vbuf[pk], 0.0, ALU.mult, ALU.add),
                                   wait=w, inc=[(s_vec, 1)], tag="vscan")
                        else:
                            P.emit("vector", lambda e, pk=pk, hl=hl: e.tensor_tensor_scan(
                                    hbuf[pk], abuf[pk], vbuf[pk], hl, ALU.mult, ALU.add),
                                   wait=w, inc=[(s_vec, 1)], tag="vscan")
                        stt["a_free"][pk] = P.cur(s_vec)
                        stt["v_use"][pk] = P.cur(s_vec)
                        scan_v = P.cur(s_vec)
                        if th == 0:
                            P.emit("vector", lambda e, hl=hl, pk=pk: e.tensor_copy(out=hl, in_=hbuf[pk][:, TH - 1:TH]),
                                   wait=[(s_vec, P.cur(s_vec))], inc=[(s_vec, 1)], tag="vhl")
                        pend.append((pk, nb, n, scan_v))
                    it += 1
                while pend:
                    emit_g(*pend.pop(0))
                pe_y = P.cur(s_pe)
                st0 = P.cur(s_st)
                stt["ev_y_pk"] = [st0 + 32, st0 + 64]
                evacs.append(make_evac(j0, dp, pe_y, st0))

    while evacs:
        evacs.pop(0)()
    P.scope = "x2"
    st_yall = P.cur(s_st)
    if g.get("tap_yslab") is not None:
        P.emit("sync", lambda e: e.dma_start(out=g["tap_yslab"][:, :, :], in_=a2a2_in[:, :, :]),
               wait=[(s_st, st_yall)], inc=[(s_st, 16)])
    def a2a2(e):
        with allow_ici_all_to_all():
            return e.collective_compute(
                "AllToAll", ALU.bypass, replica_groups=[list(range(NCORE))],
                ins=[a2a2_in.ap().rearrange("g t d -> (g t) d")],
                outs=[a2a2_out.ap().rearrange("g t d -> (g t) d")])
    P.emit("gpsimd", a2a2, wait=[(s_st, st_yall)], inc=[(s_cc, 1)], ser=False)
    env["cc_x2"] = P.cur(s_cc)
    pool.pop()


def emit_phase23(nc, P, pool, bank, env):
    g = env
    s_ld, s_gld, s_vec, s_act, s_pe, s_cc, s_st, s_ev = (
        g["s_ld"], g["s_gld"], g["s_vec"], g["s_act"], g["s_pe"], g["s_cc"], g["s_st"], g["s_ev"])
    s_gc, s_ym, s_xr, s_p4, s_w = g["s_gc"], g["s_ym"], g["s_xr"], g["s_p4"], g["s_w"]
    dvrow = g["dvrow"]
    x_own, out2_h = g["x_own"], g["out2_h"]
    a2a2_out, ar3_in, ar3_out, rt_bounce = g["a2a2_out"], g["ar3_in"], g["ar3_out"], g["rt_bounce"]
    lg_bounce = g["lg_bounce"]
    out1_scr, ffn_scr = g["out1_scr"], g["ffn_scr"]
    w_T, ffnb_s, aallT, bstkT, rwT, rbias_s = g["w_T"], g["ffnb_s"], g["aallT"], g["bstkT"], g["rwT"], g["rbias_s"]
    ident_b, ident_r, qmask_b = g["ident_b"], g["ident_r"], g["qmask_b"]
    cst = g["cst"]
    cc_x2 = env["cc_x2"]
    tap_ym, tap_out1, tap_xln3, tap_rt, tap_t1s = g["tap_ym"], g["tap_out1"], g["tap_xln3"], g["tap_rt"], g["tap_t1s"]
    tap_lgx, tap_qsf, tap_qsb = g["tap_lgx"], g["tap_qsf"], g["tap_qsb"]

    P.scope = "p2"
    pool.push()
    Db = pool.take(DM, BF16, name="Db")
    xtmp = pool.take(DM // 2, BF16, name="xtmp")
    ymb = [pool.take(DM, F32, name="ymb") for _ in range(2)]
    xrb = [pool.take(DM, F32, name="xrb") for _ in range(2)]
    st48 = pool.take(48, F32, name="st48b")
    mv2 = pool.take(2, F32, name="mv2")
    rstd2 = pool.take(2, F32, name="rstd2")
    o1b = pool.take(DM, F32, name="o1b")
    xln3T = pool.take(32 * TPC, BF16, name="xln3T")
    qsb = pool.take(32, BF16, name="qsb")
    qsf = pool.take(32, F32, name="qsf")
    lgt = pool.take(1, F32, parts=8, name="lgt")
    lrow = pool.take(8, F32, parts=1, name="lrow")
    smax = pool.take(4, F32, parts=1, name="smax")
    rt128 = pool.take(1, F32, name="rt128")
    t1s = pool.take(TPC, BF16, name="t1s")
    aall_b = pool.take(32 * 128, BF16, name="aall_b")
    bstk_b = pool.take(32 * 128, BF16, name="bstk_b")
    rwT_b = pool.take(32 * 8, BF16, name="rwT_b")
    pool.push()
    xln3_b = [pool.take(DM, BF16, name="xln3_b") for _ in range(2)]
    pool.pop()
    wslab = [pool.take(32 * 128, BF16, name="wslab") for _ in range(2)]
    fsb = g["fsb_t"].ap()
    fstg = ymb[0][:, 0:TPC]
    f4 = ymb
    o4 = xrb
    ob = o1b

    P.emit("gpsimd", lambda e: e.dma_start(out=Db,
            in_=dvrow.partition_broadcast(128).rearrange("p o t -> p (o t)")), inc=[(s_gc, 16)])
    P.emit("gpsimd", lambda e: e.dma_start(out=aall_b.rearrange("p (c m) -> p c m", m=128), in_=aallT.rearrange("(c p) m -> p c m", p=128)), inc=[(s_gc, 16)])
    P.emit("gpsimd", lambda e: e.dma_start(out=bstk_b, in_=bstkT[:, :]), inc=[(s_gc, 16)])
    P.emit("gpsimd", lambda e: e.dma_start(out=rwT_b.rearrange("p (c m) -> p c m", m=8), in_=rwT.rearrange("(c p) m -> p c m", p=128)), inc=[(s_gc, 16)])
    g_wts = P.cur(s_gc)

    pTq = bank[4].ap()[:, 0:32]
    pT3 = [bank[6].ap().bitcast(BF16), bank[7].ap().bitcast(BF16)]

    v_free = [0, 0]
    pe_x3 = [0, 0]
    for tt in range(8):
        bi = tt % 2
        for gg in range(NCORE):
            w = []
            if tt == 0 and gg == 0:
                w = [(s_cc, cc_x2)]
            if tt >= 2 and gg == 0:
                w = [(s_vec, v_free[bi])]
            P.emit("gpsimd", lambda e, tt=tt, gg=gg, bi=bi: e.dma_start(
                    out=ymb[bi][:, gg * DSH:(gg + 1) * DSH],
                    in_=a2a2_out[gg, tt * 128:(tt + 1) * 128, :]),
                   wait=w, inc=[(s_ym[bi], 16)])
        g_ym = P.cur(s_ym[bi])
        P.emit("sync", lambda e, tt=tt, bi=bi: e.dma_start(out=xrb[bi], in_=x_own[tt * 128:(tt + 1) * 128, :]),
               wait=[(s_vec, v_free[bi])] if tt >= 2 else [], inc=[(s_xr[bi], 16)])
        ld_x = P.cur(s_xr[bi])
        if tap_ym is not None:
            P.emit("sync", lambda e, tt=tt, bi=bi: e.dma_start(out=tap_ym[tt * 128:(tt + 1) * 128, :], in_=ymb[bi]),
                   wait=[(s_ym[bi], g_ym)], inc=[(s_st, 16)])

        def statsx(e, bi=bi):
            last = None
            for c in range(8):
                last = e.bn_stats(st48[:, c * 6:(c + 1) * 6], xrb[bi][:, c * 512:(c + 1) * 512])
            return last
        P.emit("vector", statsx, wait=[(s_xr[bi], ld_x)], inc=[(s_vec, 1)], tag="v2")
        P.emit("vector", lambda e: e.bn_aggr(mv2, st48), wait=[(s_vec, P.cur(s_vec))], inc=[(s_vec, 1)], tag="v2")
        P.emit("scalar", lambda e: e.activation(out=rstd2[:, 0:1], in_=mv2[:, 1:2], func=AF.Ln, bias=cst[:, 0:1]),
               wait=[(s_vec, P.cur(s_vec))], inc=[(s_act, 1)])
        P.emit("scalar", lambda e: e.activation(out=rstd2[:, 1:2], in_=rstd2[:, 0:1], func=AF.Exp, scale=cst[:, 1:2]),
               wait=[(s_act, P.cur(s_act))], inc=[(s_act, 1)])
        for hh in range(2):
            hs = slice(hh * (DM // 2), (hh + 1) * (DM // 2))
            P.emit("vector", lambda e, bi=bi, hs=hs: e.tensor_scalar(xtmp, xrb[bi][:, hs], mv2[:, 0:1], rstd2[:, 1:2], ALU.subtract, ALU.mult),
                   wait=[(s_act, P.cur(s_act)), (s_vec, P.cur(s_vec))], inc=[(s_vec, 1)], tag="v2")
            P.emit("vector", lambda e, hs=hs: e.tensor_mul(xtmp, xtmp, Db[:, hs]),
                   wait=[(s_vec, P.cur(s_vec)), (s_gc, g_wts)], inc=[(s_vec, 1)], tag="v2")
            P.emit("vector", lambda e, bi=bi, hs=hs: e.tensor_add(ymb[bi][:, hs], ymb[bi][:, hs], xtmp),
                   wait=[(s_vec, P.cur(s_vec)), (s_ym[bi], g_ym)], inc=[(s_vec, 1)], tag="v2")
        def stats2(e, bi=bi):
            last = None
            for c in range(8):
                last = e.bn_stats(st48[:, c * 6:(c + 1) * 6], ymb[bi][:, c * 512:(c + 1) * 512])
            return last
        P.emit("vector", stats2, wait=[(s_vec, P.cur(s_vec))], inc=[(s_vec, 1)], tag="v2")
        P.emit("vector", lambda e: e.bn_aggr(mv2, st48), wait=[(s_vec, P.cur(s_vec))], inc=[(s_vec, 1)], tag="v2")
        P.emit("scalar", lambda e: e.activation(out=rstd2[:, 0:1], in_=mv2[:, 1:2], func=AF.Ln, bias=cst[:, 0:1]),
               wait=[(s_vec, P.cur(s_vec))], inc=[(s_act, 1)])
        P.emit("scalar", lambda e: e.activation(out=rstd2[:, 1:2], in_=rstd2[:, 0:1], func=AF.Exp, scale=cst[:, 1:2]),
               wait=[(s_act, P.cur(s_act))], inc=[(s_act, 1)])
        a_r2 = P.cur(s_act)
        P.emit("vector", lambda e, bi=bi: e.tensor_scalar(o1b, ymb[bi], mv2[:, 0:1], rstd2[:, 1:2], ALU.subtract, ALU.mult),
               wait=[(s_act, a_r2), (s_xr[bi], ld_x)] + ([(s_st, st_o1)] if tt >= 1 else []), inc=[(s_vec, 1)], tag="v2")
        P.emit("vector", lambda e, bi=bi: e.tensor_add(o1b, o1b, xrb[bi]), wait=[(s_vec, P.cur(s_vec))], inc=[(s_vec, 1)], tag="v2")
        v_o1 = P.cur(s_vec)
        v_free[bi] = v_o1
        P.emit("sync", lambda e, tt=tt: e.dma_start(out=out1_scr[tt * 128:(tt + 1) * 128, :], in_=o1b),
               wait=[(s_vec, v_o1)], inc=[(s_st, 16)])
        st_o1 = P.cur(s_st)
        if tap_out1 is not None:
            P.emit("sync", lambda e, tt=tt: e.dma_start(out=tap_out1[tt * 128:(tt + 1) * 128, :], in_=o1b),
                   inc=[(s_st, 16)])
        def stats3(e):
            last = None
            for c in range(8):
                last = e.bn_stats(st48[:, c * 6:(c + 1) * 6], o1b[:, c * 512:(c + 1) * 512])
            return last
        P.emit("vector", stats3, wait=[(s_vec, v_o1)], inc=[(s_vec, 1)], tag="v2")
        P.emit("vector", lambda e: e.bn_aggr(mv2, st48), wait=[(s_vec, P.cur(s_vec))], inc=[(s_vec, 1)], tag="v2")
        P.emit("scalar", lambda e: e.activation(out=rstd2[:, 0:1], in_=mv2[:, 1:2], func=AF.Ln, bias=cst[:, 0:1]),
               wait=[(s_vec, P.cur(s_vec))], inc=[(s_act, 1)])
        P.emit("scalar", lambda e: e.activation(out=rstd2[:, 1:2], in_=rstd2[:, 0:1], func=AF.Exp, scale=cst[:, 1:2]),
               wait=[(s_act, P.cur(s_act))], inc=[(s_act, 1)])
        w = [(s_act, P.cur(s_act))]
        if tt >= 2:
            w.append((s_pe, pe_x3[bi]))
        P.emit("vector", lambda e, bi=bi: e.tensor_scalar(xln3_b[bi], o1b, mv2[:, 0:1], rstd2[:, 1:2], ALU.subtract, ALU.mult),
               wait=w, inc=[(s_vec, 1)], tag="v2")
        v_x3 = P.cur(s_vec)
        if tap_xln3 is not None:
            P.emit("gpsimd", lambda e, tt=tt, bi=bi: e.dma_start(out=tap_xln3[tt * 128:(tt + 1) * 128, :], in_=xln3_b[bi]),
                   wait=[(s_vec, v_x3)], inc=[(s_gld, 16)])
        def qmm(e, bi=bi, tt=tt):
            last = None
            for c in range(32):
                last = e.matmul(pTq[:, c:c + 1], xln3_b[bi][:, c * 128:(c + 1) * 128],
                                qmask_b[:, tt:tt + 1], start=(tt == 0 and c == 0), stop=(tt == 7 and c == 31))
            return last
        P.emit("tensor", qmm, wait=[(s_vec, v_x3)], inc=[(s_pe, 1)])
        for g4 in range(8):
            pbi = g4 % 2
            def transp3(e, bi=bi, g4=g4, pbi=pbi):
                last = None
                for c4 in range(4):
                    c = g4 * 4 + c4
                    last = e.matmul(pT3[pbi][:, c4 * 128:(c4 + 1) * 128],
                                    xln3_b[bi][:, c * 128:(c + 1) * 128],
                                    ident_b, is_transpose=True, start=True, stop=True)
                return last
            w = [(s_vec, v_x3)]
            if tt > 0 or g4 >= 2:
                w.append((s_act, a_e3[pbi]))
            P.emit("tensor", transp3, wait=w, inc=[(s_pe, 1)])
            if g4 == 7:
                pe_x3[bi] = P.cur(s_pe)
            pe_t = P.cur(s_pe)
            if tt == 0 and g4 == 0:
                a_e3 = [0, 0]
            def ev3(e, tt=tt, g4=g4, pbi=pbi):
                last = None
                for c4 in range(4):
                    c = g4 * 4 + c4
                    last = e.copy(out=xln3T.rearrange("p (c t) -> p c t", c=32)[:, c, tt * 128:(tt + 1) * 128],
                                  in_=pT3[pbi][:, c4 * 128:(c4 + 1) * 128])
                return last
            P.emit("scalar", ev3, wait=[(s_pe, pe_t)], inc=[(s_act, 1)])
            a_e3[pbi] = P.cur(s_act)
    a_x3all = P.cur(s_act)
    pe_all = P.cur(s_pe)
    if MAX_PHASE < 3:
        pool.pop()
        return
    P.emit("scalar", lambda e: e.copy(out=qsf, in_=pTq), wait=[(s_pe, pe_all)], inc=[(s_act, 1)])
    a_qe = P.cur(s_act)
    if tap_qsf is not None:
        P.emit("sync", lambda e: e.dma_start(out=tap_qsf[:, :], in_=qsf), wait=[(s_act, a_qe)], inc=[(s_st, 16)])
    P.emit("sync", lambda e: e.dma_start(out=ar3_in[:, :], in_=qsf),
           wait=[(s_act, a_qe)], inc=[(s_st, 16)])
    st_q = P.cur(s_st)
    P.emit("gpsimd", lambda e: e.collective_compute(
            "AllReduce", ALU.add, replica_groups=[[0, 1, 2, 3], [4, 5, 6, 7]],
            ins=[ar3_in[:, :]], outs=[ar3_out[:, :]]),
           wait=[(s_st, st_q)], inc=[(s_cc, 1)], ser=False)
    cc_q = P.cur(s_cc)
    P.emit("gpsimd", lambda e: e.dma_start(out=qsb, in_=ar3_out[:, :]), wait=[(s_cc, cc_q)], inc=[(s_gld, 16)])
    g_q = P.cur(s_gld)
    if tap_qsb is not None:
        P.emit("gpsimd", lambda e: e.dma_start(out=tap_qsb[:, :], in_=qsb), wait=[(s_gld, g_q)], inc=[(s_gld, 16)])

    def lmm(e):
        last = None
        for c in range(32):
            last = e.matmul(bank[5].ap()[0:8, 0:1], rwT_b.rearrange("p (c m) -> p c m", c=32)[:, c, :],
                            qsb[:, c:c + 1], start=(c == 0), stop=(c == 31))
        return last
    P.emit("tensor", lmm, wait=[(s_gld, g_q), (s_gc, g_wts)], inc=[(s_pe, 1)])
    P.emit("scalar", lambda e: e.activation(out=lgt, in_=bank[5].ap()[0:8, 0:1], func=AF.Identity, bias=rbias_s),
           wait=[(s_pe, P.cur(s_pe))], inc=[(s_act, 1)])
    a_l = P.cur(s_act)
    P.emit("sync", lambda e: e.dma_start(out=lg_bounce[:, :], in_=lgt), wait=[(s_act, a_l)], inc=[(s_st, 16)])
    st_lg = P.cur(s_st)
    P.emit("sync", lambda e: e.dma_start(out=lrow, in_=lg_bounce.ap().rearrange("e o -> o e")), wait=[(s_st, st_lg)], inc=[(s_ld, 16)])
    a_lr = P.cur(s_ld)
    if tap_lgx is not None:
        P.emit("sync", lambda e: e.dma_start(out=tap_lgx[:, :], in_=lrow), wait=[(s_ld, a_lr)], inc=[(s_st, 16)])
    P.emit("vector", lambda e: e.reduce_max(smax[:, 0:1], lrow, axis=mybir.AxisListType.X), wait=[(s_ld, a_lr)], inc=[(s_vec, 1)])
    P.emit("vector", lambda e: e.tensor_scalar_mul(smax[:, 1:2], smax[:, 0:1], -1.0), wait=[(s_vec, P.cur(s_vec))], inc=[(s_vec, 1)])
    P.emit("scalar", lambda e: e.activation(out=lrow, in_=lrow, func=AF.Exp, bias=smax[:, 1:2]),
           wait=[(s_vec, P.cur(s_vec))], inc=[(s_act, 1)])
    P.emit("vector", lambda e: e.reduce_sum(smax[:, 2:3], lrow, axis=mybir.AxisListType.X),
           wait=[(s_act, P.cur(s_act))], inc=[(s_vec, 1)])
    P.emit("vector", lambda e: e.reciprocal(smax[:, 3:4], smax[:, 2:3]), wait=[(s_vec, P.cur(s_vec))], inc=[(s_vec, 1)])
    P.emit("vector", lambda e: e.tensor_scalar(lrow, lrow, smax[:, 3:4], 2.0, ALU.mult, ALU.mult), wait=[(s_vec, P.cur(s_vec))], inc=[(s_vec, 1)])
    v_rt = P.cur(s_vec)
    if tap_rt is not None:
        P.emit("sync", lambda e: e.dma_start(out=tap_rt[:, :], in_=lrow), wait=[(s_vec, v_rt)], inc=[(s_st, 16)])
    P.emit("sync", lambda e: e.dma_start(out=rt_bounce[:, :], in_=lrow), wait=[(s_vec, v_rt)], inc=[(s_st, 16)])
    st_rt = P.cur(s_st)
    P.emit("gpsimd", lambda e: e.dma_start(out=rt128,
            in_=bass.AP(rt_bounce, 0, [[1, 8], [0, 16]]),
            allow_slow_non_contiguous=True),
           wait=[(s_st, st_rt)], inc=[(s_gld, 16)])
    g_rt = P.cur(s_gld)
    if MAX_PHASE < 4:
        pool.pop()
        return

    def t1mm(e):
        a3 = aall_b.rearrange("p (c m) -> p c m", c=32)
        x3 = xln3T.rearrange("p (c t) -> p c t", c=32)
        last = None
        for c in range(32):
            e.matmul(bank[0].ap(), a3[:, c, :], x3[:, c, 0:512], start=(c == 0), stop=(c == 31))
            last = e.matmul(bank[1].ap(), a3[:, c, :], x3[:, c, 512:1024], start=(c == 0), stop=(c == 31))
        return last
    P.emit("tensor", t1mm, wait=[(s_act, a_x3all), (s_gc, g_wts)], inc=[(s_pe, 1)])
    pe_t1 = P.cur(s_pe)
    P.emit("vector", lambda e: (e.tensor_scalar(t1s[:, 0:512], bank[0].ap(), rt128, None, ALU.mult),
                                e.tensor_scalar(t1s[:, 512:1024], bank[1].ap(), rt128, None, ALU.mult))[-1],
           wait=[(s_pe, pe_t1), (s_gld, g_rt)], inc=[(s_vec, 1)])
    v_t1 = P.cur(s_vec)
    if tap_t1s is not None:
        P.emit("sync", lambda e: e.dma_start(out=tap_t1s[:, :], in_=t1s), wait=[(s_vec, v_t1)], inc=[(s_st, 16)])

    a_fs = 0
    st_f = 0
    pe_fb = [0, 0]
    a_wfree = [0, 0]
    for oc in range(32):
        wb = oc % 2
        w = [(s_pe, pe_fb[wb])] if pe_fb[wb] else [(s_pe, pe_all), (s_act, a_x3all)]
        P.emit("gpsimd", lambda e, oc=oc, wb=wb: e.dma_start(
                out=wslab[wb].rearrange("p (c o) -> p c o", o=128), in_=w_T[:, oc * 128:(oc + 1) * 128].rearrange("(c p) o -> p c o", p=128)),
               wait=w, inc=[(s_w[wb], 16)])
        g_w = P.cur(s_w[wb])
        pb0 = 2 if (oc % 2 == 0) else 4
        def ffnmm(e, oc=oc, wb=wb, pb0=pb0):
            ws = wslab[wb].rearrange("p (c o) -> p c o", c=32)
            x3 = xln3T.rearrange("p (c t) -> p c t", c=32)
            for c in range(32):
                e.matmul(bank[pb0].ap(), ws[:, c, :], x3[:, c, 0:512], start=(c == 0), stop=False)
                e.matmul(bank[pb0 + 1].ap(), ws[:, c, :], x3[:, c, 512:1024], start=(c == 0), stop=False)
            bs = bstk_b.rearrange("p (c o) -> p c o", c=32)
            e.matmul(bank[pb0].ap(), bs[:, oc, :], t1s[:, 0:512], start=False, stop=True)
            return e.matmul(bank[pb0 + 1].ap(), bs[:, oc, :], t1s[:, 512:1024], start=False, stop=True)
        w = [(s_w[wb], g_w), (s_vec, v_t1)]
        if oc == 1:
            w.append((s_act, a_qe))
        if a_fs and oc >= 2:
            w.append((s_act, a_fsb[wb]))
        P.emit("tensor", ffnmm, wait=w, inc=[(s_pe, 1)])
        pe_fb[wb] = P.cur(s_pe)
        if oc == 0:
            a_fsb = [0, 0]
        def evf(e, oc=oc, pb0=pb0):
            e.activation(out=fsb[:, 0:512], in_=bank[pb0].ap(), func=AF.Identity, bias=ffnb_s[:, oc:oc + 1])
            return e.activation(out=fsb[:, 512:1024], in_=bank[pb0 + 1].ap(), func=AF.Identity, bias=ffnb_s[:, oc:oc + 1])
        P.emit("scalar", evf, wait=[(s_pe, pe_fb[wb])] + ([(s_pe, pe_tb)] if oc > 0 else []), inc=[(s_act, 1)])
        a_fs = P.cur(s_act)
        a_fsb[wb] = a_fs
        def tback(e, oc=oc):
            last = None
            for tb in range(8):
                bk = 6 + tb // 4
                col = (tb % 4) * 128
                last = e.matmul(bank[bk].ap().bitcast(F32R)[:, col:col + 128],
                                fsb[:, tb * 128:(tb + 1) * 128],
                                ident_r, is_transpose=True, start=True, stop=True)
            return last
        w = [(s_act, a_fs)]
        if oc > 0:
            w.append((s_act, a_ftg))
        P.emit("tensor", tback, wait=w, inc=[(s_pe, 1)])
        pe_tb = P.cur(s_pe)
        P.emit("scalar", lambda e: (e.copy(out=fstg[:, 0:512], in_=bank[6].ap()),
                                    e.copy(out=fstg[:, 512:1024], in_=bank[7].ap()))[-1],
               wait=[(s_pe, pe_tb)] + ([(s_st, st_f)] if st_f else []), inc=[(s_act, 1)])
        a_ftg = P.cur(s_act)
        P.emit("sync", lambda e, oc=oc: e.dma_start(
                out=ffn_scr.ap()[:, oc * 128:(oc + 1) * 128].rearrange("(q p) c -> p q c", p=128),
                in_=fstg.rearrange("p (q c) -> p q c", c=128)),
               wait=[(s_act, a_ftg)], inc=[(s_st, 16)])
        st_f = P.cur(s_st)

    if MAX_PHASE < 5:
        pool.pop()
        return
    P.scope = "p4"
    st_out = 0
    for tt in range(8):
        bi = tt % 2
        w = [(s_st, st_f)]
        if tt >= 2:
            w = [(s_vec, v_out[bi])]
        P.emit("sync", lambda e, tt=tt, bi=bi: e.dma_start(out=f4[bi], in_=ffn_scr[tt * 128:(tt + 1) * 128, :]),
               wait=w, inc=[(s_p4[bi], 16)])
        P.emit("sync", lambda e, tt=tt, bi=bi: e.dma_start(out=o4[bi], in_=out1_scr[tt * 128:(tt + 1) * 128, :]),
               inc=[(s_p4[bi], 16)])
        ld_f = P.cur(s_p4[bi])
        def stats4(e, bi=bi):
            last = None
            for c in range(8):
                last = e.bn_stats(st48[:, c * 6:(c + 1) * 6], f4[bi][:, c * 512:(c + 1) * 512])
            return last
        P.emit("vector", stats4, wait=[(s_p4[bi], ld_f)], inc=[(s_vec, 1)])
        P.emit("vector", lambda e: e.bn_aggr(mv2, st48), wait=[(s_vec, P.cur(s_vec))], inc=[(s_vec, 1)], tag="v2")
        P.emit("scalar", lambda e: e.activation(out=rstd2[:, 0:1], in_=mv2[:, 1:2], func=AF.Ln, bias=cst[:, 0:1]),
               wait=[(s_vec, P.cur(s_vec))], inc=[(s_act, 1)])
        P.emit("scalar", lambda e: e.activation(out=rstd2[:, 1:2], in_=rstd2[:, 0:1], func=AF.Exp, scale=cst[:, 1:2]),
               wait=[(s_act, P.cur(s_act))], inc=[(s_act, 1)])
        w = [(s_act, P.cur(s_act))]
        if tt >= 1:
            w.append((s_st, st_ob))
        P.emit("vector", lambda e, bi=bi: e.tensor_scalar(ob, f4[bi], mv2[:, 0:1], rstd2[:, 1:2], ALU.subtract, ALU.mult),
               wait=w, inc=[(s_vec, 1)])
        P.emit("vector", lambda e, bi=bi: e.tensor_add(ob, ob, o4[bi]), wait=[(s_vec, P.cur(s_vec))], inc=[(s_vec, 1)])
        v_out = v_out if tt > 0 else [0, 0]
        v_out[bi] = P.cur(s_vec)
        P.emit("sync", lambda e, tt=tt: e.dma_start(out=out2_h[tt * 128:(tt + 1) * 128, :], in_=ob),
               wait=[(s_vec, v_out[bi])], inc=[(s_st, 16)])
        st_ob = P.cur(s_st)
        st_out = st_ob
    pool.pop()


def _prep_inputs(inputs):
    x = np.ascontiguousarray(inputs["L0_attn_x"], dtype=np.float32)
    A_log = np.ascontiguousarray(inputs["L0_attn_A_log"], dtype=np.float32)
    Dv = np.ascontiguousarray(inputs["L0_attn_D"], dtype=np.float32)
    xpw = np.ascontiguousarray(inputs["L0_attn_x_proj_w"], dtype=np.float32)
    dtpw = np.ascontiguousarray(inputs["L0_attn_dt_proj_w"], dtype=np.float32)
    dtpbv = np.ascontiguousarray(inputs["L0_attn_dt_proj_b"], dtype=np.float32)
    W = np.ascontiguousarray(inputs["L0_ffn_W"], dtype=np.float32)
    fb = np.ascontiguousarray(inputs["L0_ffn_b"], dtype=np.float32)
    Ae = np.ascontiguousarray(inputs["L0_ffn_A"], dtype=np.float32)
    Be = np.ascontiguousarray(inputs["L0_ffn_B"], dtype=np.float32)
    rW = np.ascontiguousarray(inputs["L0_ffn_router_W"], dtype=np.float32)
    rb = np.ascontiguousarray(inputs["L0_ffn_router_b"], dtype=np.float32)

    xpwT_h = np.ascontiguousarray(xpw.T)
    w_T_h = np.ascontiguousarray(W.T)
    aallT_h = np.ascontiguousarray(Ae.transpose(2, 0, 1).reshape(128 * NST // 16, -1).T
                                   if False else Ae.transpose(2, 0, 1).reshape(-1, 128))
    aallT_h = np.ascontiguousarray(Ae.transpose(2, 0, 1).reshape(4096, 128))
    bstkT_h = np.ascontiguousarray(Be.transpose(0, 2, 1).reshape(128, 4096))
    rwT_h = np.ascontiguousarray(rW.T)
    ident = np.eye(128, dtype=np.float32)
    ffnb_h = np.ascontiguousarray(fb.reshape(32, 128).T)

    in_maps = []
    for k in range(NCORE):
        beta = k // 4
        t0 = (k % 4) * TPC
        sh = slice(DSH * k, DSH * (k + 1))
        dtpwT_h = np.ascontiguousarray(dtpw[sh, :].T)
        dtpb_h = np.ascontiguousarray(dtpbv[sh].reshape(NDT, 128).T)
        alog_h = np.ascontiguousarray(A_log[sh].reshape(NDT, 128, NST).transpose(1, 0, 2).reshape(128, NDT * NST))
        dvec_h = np.ascontiguousarray(Dv[sh].reshape(NDT, 128).T)
        gidx = np.arange(t0, t0 + TPC)
        qm = ((gidx >= Q0) & (gidx < Q1)).astype(np.float32) / float(Q1 - Q0)
        in_maps.append({
            "x_own": np.ascontiguousarray(x[beta, t0:t0 + TPC, :]),
            "xpwT": xpwT_h, "dtpwT": dtpwT_h, "dtpb": dtpb_h, "alog": alog_h,
            "dvecs": dvec_h, "w_T": w_T_h, "ffnb": ffnb_h, "aallT": aallT_h,
            "bstkT": bstkT_h, "rwT": rwT_h, "rbias": rb.reshape(8, 1),
            "identf": ident, "qmask": np.ascontiguousarray(qm.reshape(8, 128).T), "dvrow": Dv.reshape(1, 4096),
        })
    return in_maps


def kernel(**inputs):
    in_maps = _prep_inputs(inputs)
    nc, taps = build_program()
    res = run_bass_kernel_spmd(nc, in_maps, core_ids=list(range(NCORE)))
    out = np.empty((2, 4096, 4096), dtype=np.float32)
    for k in range(NCORE):
        beta = k // 4
        t0 = (k % 4) * TPC
        out[beta, t0:t0 + TPC, :] = res.results[k]["out2"]
    kernel._last_results = res
    return out


# version 10
# speedup vs baseline: 1.0253x; 1.0253x over previous
import contextlib
import numpy as np
import concourse.bass as bass
import concourse.mybir as mybir
from concourse.bass_utils import run_bass_kernel_spmd
from concourse.aot_env import allow_ici_all_to_all

F32 = mybir.dt.float32
F32R = mybir.dt.float32r
BF16 = mybir.dt.bfloat16
U8 = mybir.dt.uint8
AF = mybir.ActivationFunctionType
ALU = mybir.AluOpType

NCORE = 8
DM, NST, RDT = 4096, 16, 128
TPC = 1024
DSH = 512
NDT = DSH // 128
LB = 4096
TH = 2048
Q0, Q1 = 611, 4095
EPS = 1e-5

DEBUG_TAPS = ()
MAX_PHASE = 99
SERIALIZE = 0
SER_ENGINES = ("sync", "gpsimd", "vector", "scalar", "tensor")
SER_EXEMPT = ()


class Plan:
    def __init__(self):
        self.items = []
        self.val = {}
        self.ser_sem = None
        self.scope = None

    def emit(self, engine, fn, wait=(), inc=(), ser=True, tag=""):
        if SERIALIZE and ser and engine in SER_ENGINES and tag not in SER_EXEMPT:
            wait = tuple(wait) + tuple((sm, v) for sm, v in self.val.items())
        for s, a in inc:
            self.val[s] = self.val.get(s, 0) + a
        self.items.append((engine, fn, tuple(wait), tuple(inc), self.scope))

    def cur(self, sem):
        return self.val.get(sem, 0)

    def replay(self, engine_name, eng, nc=None):
        import contextlib as _ctx
        cur_scope = None
        guard = None
        for item in self.items:
            e, fn, waits, incs = item[:4]
            scope = item[4] if len(item) > 4 else None
            if e != engine_name:
                continue
            if nc is not None and scope != cur_scope:
                if guard is not None:
                    guard.__exit__(None, None, None)
                    guard = None
                if scope is not None:
                    guard = nc.named_scope(scope)
                    guard.__enter__()
                cur_scope = scope
            for s, v in waits:
                if v > 0:
                    eng.wait_ge(s, v)
            inst = fn(eng)
            for s, a in incs:
                inst = inst.then_inc(s, a)
        if guard is not None:
            guard.__exit__(None, None, None)


class SbufPool:
    def __init__(self, nc, cx, name, nbytes):
        self.t = cx.enter_context(nc.sbuf_tensor(name, [128, nbytes], U8))
        self.n = nbytes
        self.off = 0
        self.marks = []
        self.hi = 0

    def take(self, cols, dt, parts=128, name=""):
        esz = mybir.dt.size(dt)
        off = (self.off + 3) // 4 * 4
        nb = cols * esz
        assert off + nb <= self.n, f"SbufPool overflow at {name}: {(off+nb)//1024}KB > {self.n//1024}KB"
        self.off = off + nb
        self.hi = max(self.hi, self.off)
        return self.t.ap()[0:parts, off:off + nb].bitcast(dt)

    def push(self):
        self.marks.append(self.off)

    def pop(self):
        self.off = self.marks.pop()


def build_program():
    nc = bass.Bass("TRN2", target_bir_lowering=False)
    dram = lambda n, sh, dt=F32, **kw: nc.dram_tensor(n, sh, dt, **kw)

    x_own = dram("x_own", [TPC, DM], kind="ExternalInput").ap()
    xpwT = dram("xpwT", [DM, 160], kind="ExternalInput").ap()
    dtpwT = dram("dtpwT", [RDT, DSH], kind="ExternalInput").ap()
    dtpb = dram("dtpb", [128, NDT], kind="ExternalInput").ap()
    alog = dram("alog", [128, NDT * NST], kind="ExternalInput").ap()
    dvecs = dram("dvecs", [128, NDT], kind="ExternalInput").ap()
    w_T = dram("w_T", [DM, DM], kind="ExternalInput").ap()
    ffnb = dram("ffnb", [128, 32], kind="ExternalInput").ap()
    aallT = dram("aallT", [DM, 128], kind="ExternalInput").ap()
    bstkT = dram("bstkT", [128, DM], kind="ExternalInput").ap()
    rwT = dram("rwT", [DM, 8], kind="ExternalInput").ap()
    rbias = dram("rbias", [8, 1], kind="ExternalInput").ap()
    identf = dram("identf", [128, 128], kind="ExternalInput").ap()
    qmask = dram("qmask", [128, 8], kind="ExternalInput").ap()
    dvrow = dram("dvrow", [1, DM], kind="ExternalInput").ap()
    out2_h = dram("out2", [TPC, DM], kind="ExternalOutput").ap()

    a2a1_in = dram("a2a1_in", [NCORE, DSH, TPC], BF16)
    a2a1_out = dram("a2a1_out", [NCORE, DSH, TPC], BF16)
    ag1_in = dram("ag1_in", [160, TPC], BF16)
    ag1_out = dram("ag1_out", [NCORE * 160, TPC], BF16)
    a2a2_in = dram("a2a2_in", [NCORE, TPC, DSH])
    a2a2_out = dram("a2a2_out", [NCORE, TPC, DSH])
    ar3_in = dram("ar3_in", [128, 32])
    ar3_out = dram("ar3_out", [128, 32])
    rt_bounce = dram("rt_bounce", [1, 8])
    lg_bounce = dram("lg_bounce", [8, 1])
    out1_scr = dram("out1_scr", [TPC, DM])
    ffn_scr = dram("ffn_scr", [TPC, DM])

    taps = {}
    def tap(name, sh, dt=F32):
        if name in DEBUG_TAPS:
            taps[name] = dram("tap_" + name, sh, dt, kind="ExternalOutput").ap()
        return taps.get(name)

    tap_xp = tap("xp", [160, TPC])
    tap_xlnT = tap("xlnT", [NCORE, DSH, TPC], BF16)
    tap_dtT = tap("dtT", [DSH, 2 * LB])
    tap_u = tap("u", [DSH, 2 * LB])
    tap_ym = tap("ym", [TPC, DM])
    tap_yslab = tap("yslab", [NCORE, TPC, DSH])
    tap_ab1 = tap("ab1", [128, TH])
    tap_vb1 = tap("vb1", [128, TH])
    tap_hb1 = tap("hb1", [128, TH])
    tap_gb1 = tap("gb1", [128, TH])
    tap_yst1 = tap("yst1", [128, TH])
    tap_hl = tap("hl", [128, NST])
    tap_a2 = tap("a2", [128, TH])
    tap_v2 = tap("v2", [128, TH])
    tap_h2 = tap("h2", [128, TH])
    tap_out1 = tap("out1", [TPC, DM])
    tap_xln3 = tap("xln3", [TPC, DM])
    tap_rt = tap("rt", [1, 8])
    tap_lgx = tap("lgx", [1, 8])
    tap_qsf = tap("qsf", [128, 32])
    tap_qsb = tap("qsb", [128, 32])
    tap_t1s = tap("t1s", [128, TPC])

    cx = contextlib.ExitStack()
    with cx:
        sem = lambda n: cx.enter_context(nc.semaphore(n))
        s_ld = sem("s_ld"); s_gld = sem("s_gld"); s_vec = sem("s_vec")
        s_act = sem("s_act"); s_pe = sem("s_pe"); s_cc = sem("s_cc")
        s_st = sem("s_st"); s_ev = sem("s_ev")
        s_pl = sem("s_pl")
        s_gc = sem("s_gc")
        s_dt = sem("s_dt")
        s_xt = [sem("s_xt0"), sem("s_xt1")]
        s_x = [sem("s_x0"), sem("s_x1")]
        s_bb = [sem("s_bb0"), sem("s_bb1")]
        s_ym = [sem("s_ym0"), sem("s_ym1")]
        s_xr = [sem("s_xr0"), sem("s_xr1")]
        s_p4 = [sem("s_p40"), sem("s_p41")]
        s_w = [sem("s_w0"), sem("s_w1")]
        P_sersem = sem("s_ser")

        pool = SbufPool(nc, cx, "pool", 195 * 1024)
        sb_r = lambda n, sh: cx.enter_context(nc.sbuf_tensor(n, sh, F32R))
        gbuf_t = sb_r("gbuf_t", [128, TH])
        fsb_t = sb_r("fsb_t", [128, TPC])
        identr_t = sb_r("identr_t", [128, 128])
        bank = [cx.enter_context(nc.psum_tensor(f"bank{i}", [128, 512], F32)) for i in range(8)]

        P = Plan()
        P.ser_sem = P_sersem

        ident_b = pool.take(128, BF16, name="ident_b")
        ident_r = identr_t.ap()
        dtpb_s = pool.take(NDT, F32, name="dtpb_s")
        dvec_s = pool.take(NDT, F32, name="dvec_s")
        alog_s = pool.take(NDT * NST, F32, name="alog_s")
        adev = pool.take(NDT * NST, F32, name="adev")
        qmask_b = pool.take(8, BF16, name="qmask_b")
        rbias_s = pool.take(1, F32, parts=8, name="rbias_s")
        ffnb_s = pool.take(32, F32, name="ffnb_s")
        dtpwT_b = pool.take(DSH, BF16, parts=RDT, name="dtpwT_b")
        cst = pool.take(4, F32, name="cst")

        def init_cst(e):
            e.memset(cst[:, 0:1], float(EPS))
            e.memset(cst[:, 1:2], -0.5)
            return e.memset(cst[:, 2:3], -1.0)
        P.emit("vector", init_cst, inc=[(s_vec, 1)])
        cst_done = P.cur(s_vec)
        P.emit("sync", lambda e: e.dma_start(out=dtpb_s, in_=dtpb[:, :]), inc=[(s_ld, 16)])
        P.emit("sync", lambda e: e.dma_start(out=dvec_s, in_=dvecs[:, :]), inc=[(s_ld, 16)])
        P.emit("sync", lambda e: e.dma_start(out=alog_s, in_=alog[:, :]), inc=[(s_ld, 16)])
        P.emit("sync", lambda e: e.dma_start(out=rbias_s, in_=rbias[:, :]), inc=[(s_ld, 16)])
        P.emit("sync", lambda e: e.dma_start(out=ffnb_s, in_=ffnb[:, :]), inc=[(s_ld, 16)])

        P.emit("gpsimd", lambda e: e.dma_start(out=ident_b, in_=identf[:, :]), inc=[(s_gc, 16)])
        P.emit("gpsimd", lambda e: e.dma_start(out=ident_r, in_=identf[:, :]), inc=[(s_gc, 16)])
        P.emit("gpsimd", lambda e: e.dma_start(out=qmask_b, in_=qmask[:, :]), inc=[(s_gc, 16)])
        P.emit("gpsimd", lambda e: e.dma_start(out=dtpwT_b, in_=dtpwT[:, :]), inc=[(s_gc, 16)])
        P.emit("gpsimd", lambda e: e.dma_start(out=xpwT_b.rearrange("p (c r) -> p c r", r=160), in_=xpwT.rearrange("(c p) r -> p c r", p=128)), inc=[(s_gc, 16)])
        ld_const = P.cur(s_ld)
        gld_const = P.cur(s_gc)
        P.emit("scalar", lambda e: e.activation(out=adev, in_=alog_s, func=AF.Exp),
               wait=[(s_ld, ld_const), (s_vec, cst_done)], inc=[(s_act, 1)])
        P.emit("vector", lambda e: e.tensor_scalar_mul(adev, adev, -1.0),
               wait=[(s_act, P.cur(s_act))], inc=[(s_vec, 1)])
        adev_rdy = P.cur(s_vec)

        P.scope = "p0"
        pool.push()
        xpwT_b = pool.take(32 * 160, BF16, name="xpwT_b")
        xbuf = [pool.take(DM, F32, name="xbuf") for _ in range(2)]
        st48 = pool.take(48, F32, name="st48")
        mv01 = pool.take(2, F32, name="mv01")
        rstd0 = pool.take(2, F32, name="rstd0")
        xln_b = [pool.take(DM, BF16, name="xln_b") for _ in range(2)]
        xlnT_t = [pool.take(DM, BF16, name="xlnT_t") for _ in range(2)]
        xpst = pool.take(8 * 128, BF16, name="xpst")
        bcst = pool.take(8 * 128, BF16, parts=32, name="bcst")

        pT = [bank[6].ap().bitcast(BF16), bank[7].ap().bitcast(BF16)]
        pXP1 = bank[2].ap()[:, 0:128]
        pXP2 = bank[3].ap()[0:32, 0:128]

        v_norm = [0, 0]
        pe_xuse = [0, 0]
        ev_xlnT = [0, 0]
        for tt in range(8):
            bi = tt % 2
            w = [(s_vec, v_norm[bi])] if tt >= 2 else []
            P.emit("sync", lambda e, tt=tt, bi=bi: e.dma_start(out=xbuf[bi], in_=x_own[tt * 128:(tt + 1) * 128, :]),
                   wait=w, inc=[(s_x[bi], 16)])
            ldx = P.cur(s_x[bi])

            def stats_a(e, bi=bi):
                last = None
                for c in range(8):
                    last = e.bn_stats(st48[:, c * 6:(c + 1) * 6], xbuf[bi][:, c * 512:(c + 1) * 512])
                return last
            P.emit("vector", stats_a, wait=[(s_x[bi], ldx)], inc=[(s_vec, 1)], tag="v0")
            P.emit("vector", lambda e: e.bn_aggr(mv01, st48), wait=[(s_vec, P.cur(s_vec))], inc=[(s_vec, 1)], tag="v0")
            v_st = P.cur(s_vec)

            P.emit("scalar", lambda e: e.activation(out=rstd0[:, 0:1], in_=mv01[:, 1:2], func=AF.Ln, bias=cst[:, 0:1]),
                   wait=[(s_vec, v_st)], inc=[(s_act, 1)])
            P.emit("scalar", lambda e: e.activation(out=rstd0[:, 1:2], in_=rstd0[:, 0:1], func=AF.Exp, scale=cst[:, 1:2]),
                   wait=[(s_act, P.cur(s_act))], inc=[(s_act, 1)])

            w = [(s_act, P.cur(s_act)), (s_pe, pe_xuse[bi]), (s_vec, v_st)]
            P.emit("vector", lambda e, bi=bi: e.tensor_scalar(xln_b[bi], xbuf[bi], mv01[:, 0:1], rstd0[:, 1:2], ALU.subtract, ALU.mult),
                   wait=w, inc=[(s_vec, 1)], tag="v0")
            v_norm[bi] = P.cur(s_vec)

            for g4 in range(8):
                pbi = g4 % 2
                def transp(e, bi=bi, g4=g4, pbi=pbi):
                    last = None
                    for c4 in range(4):
                        c = g4 * 4 + c4
                        last = e.matmul(pT[pbi][:, c4 * 128:(c4 + 1) * 128],
                                        xln_b[bi][:, c * 128:(c + 1) * 128],
                                        ident_b, is_transpose=True, start=True, stop=True)
                    return last
                w = [(s_vec, v_norm[bi]), (s_gc, gld_const)]
                if tt > 0 or g4 >= 2:
                    w.append((s_act, act_ev[pbi]))
                P.emit("tensor", transp, wait=w, inc=[(s_pe, 1)])
                if g4 == 7:
                    pe_xuse[bi] = P.cur(s_pe)
                pe_t = P.cur(s_pe)
                if tt == 0 and g4 == 0:
                    act_ev = [0, 0]
                w = [(s_pe, pe_t)]
                if tt >= 2 and g4 < 2:
                    w.append((s_ev, ev_xlnT[bi]))
                P.emit("scalar", lambda e, bi=bi, g4=g4, pbi=pbi: e.copy(
                        out=xlnT_t[bi][:, g4 * 512:(g4 + 1) * 512], in_=pT[pbi][:, 0:512]),
                       wait=w, inc=[(s_act, 1)])
                act_ev[pbi] = P.cur(s_act)
            a_tile = P.cur(s_act)
            P.emit("gpsimd", lambda e, bi=bi, tt=tt: e.dma_start(
                    out=a2a1_in.ap()[:, :, tt * 128:(tt + 1) * 128].rearrange("g (q p) t -> p (g q) t", p=128),
                    in_=xlnT_t[bi].rearrange("p (c t) -> p c t", c=32)),
                   wait=[(s_act, a_tile)], inc=[(s_gld, 16)])
            g_slab = P.cur(s_gld)
            def xpmm(e, bi=bi):
                w3 = xpwT_b.rearrange("p (c r) -> p c r", r=160)
                last = None
                for c in range(32):
                    mv_ap = xlnT_t[bi][:, c * 128:(c + 1) * 128]
                    e.matmul(pXP1, w3[:, c, 0:128], mv_ap, start=(c == 0), stop=(c == 31))
                    last = e.matmul(pXP2, w3[:, c, 128:160], mv_ap, start=(c == 0), stop=(c == 31))
                return last
            P.emit("tensor", xpmm, wait=[(s_act, a_tile)], inc=[(s_pe, 1)])
            pe_xp = P.cur(s_pe)
            P.emit("scalar", lambda e, tt=tt: (e.copy(out=xpst[:, tt * 128:(tt + 1) * 128], in_=pXP1),
                                               e.copy(out=bcst[:, tt * 128:(tt + 1) * 128], in_=pXP2))[-1],
                   wait=[(s_pe, pe_xp)], inc=[(s_act, 1)])
            P.emit("gpsimd", lambda e: e.engine_nop(), wait=[(s_gld, g_slab), (s_pe, pe_xp)], inc=[(s_ev, 1)], ser=False)
            ev_xlnT[bi] = P.cur(s_ev)

        a_xp = P.cur(s_act)
        P.emit("sync", lambda e: e.dma_start(out=ag1_in[0:128, :], in_=xpst), wait=[(s_act, a_xp)], inc=[(s_st, 16)])
        P.emit("sync", lambda e: e.dma_start(out=ag1_in[128:160, :], in_=bcst), inc=[(s_st, 16)])
        st_xp = P.cur(s_st)
        gld_slabs = P.cur(s_gld)

        P.scope = "x1"
        P.emit("gpsimd", lambda e: e.collective_compute(
                "AllGather", ALU.bypass, replica_groups=[list(range(NCORE))],
                ins=[ag1_in[:, :]], outs=[ag1_out[:, :]]),
               wait=[(s_st, st_xp), (s_gld, gld_slabs)], inc=[(s_cc, 1)], ser=False)
        def a2a1(e):
            with allow_ici_all_to_all():
                return e.collective_compute(
                    "AllToAll", ALU.bypass, replica_groups=[list(range(NCORE))],
                    ins=[a2a1_in.ap().rearrange("g d t -> (g d) t")],
                    outs=[a2a1_out.ap().rearrange("g d t -> (g d) t")])
        P.emit("gpsimd", a2a1, inc=[(s_cc, 1)], ser=False)
        cc_x1 = P.cur(s_cc)
        if tap_xp is not None:
            P.emit("sync", lambda e: e.dma_start(out=tap_xp[:, :], in_=ag1_in[:, :]),
                   wait=[(s_st, st_xp)], inc=[(s_st, 16)])
        if tap_xlnT is not None:
            P.emit("gpsimd", lambda e: e.dma_start(out=tap_xlnT[:, :, :], in_=a2a1_out[:, :, :]),
                   wait=[(s_cc, cc_x1)], inc=[(s_gld, 16)])
        pool.pop()

        env = dict(locals())
        if MAX_PHASE >= 1:
            emit_phase1(nc, P, pool, bank, env)
        if MAX_PHASE >= 2:
            emit_phase23(nc, P, pool, bank, env)

        P.emit("gpsimd", lambda e: e.engine_nop(), wait=[(s_st, P.cur(s_st)), (s_gld, P.cur(s_gld))], ser=False)

        with nc.Block() as block:
            @block.sync
            def _(eng):
                P.replay("sync", eng, nc)
            @block.gpsimd
            def _(eng):
                P.replay("gpsimd", eng, nc)
            @block.vector
            def _(eng):
                P.replay("vector", eng, nc)
            @block.scalar
            def _(eng):
                P.replay("scalar", eng, nc)
            @block.tensor
            def _(eng):
                P.replay("tensor", eng, nc)

    return nc, taps


def emit_phase1(nc, P, pool, bank, env):
    g = env
    s_ld, s_gld, s_vec, s_act, s_pe, s_cc, s_st, s_ev = (
        g["s_ld"], g["s_gld"], g["s_vec"], g["s_act"], g["s_pe"], g["s_cc"], g["s_st"], g["s_ev"])
    s_gc, s_dt, s_xt, s_bb = g["s_gc"], g["s_dt"], g["s_xt"], g["s_bb"]
    s_pl = g["s_pl"]
    ag1_out, a2a1_out, a2a2_in, a2a2_out = g["ag1_out"], g["a2a1_out"], g["a2a2_in"], g["a2a2_out"]
    dtpwT_b, dtpb_s, adev, dvec_s, ident_r = g["dtpwT_b"], g["dtpb_s"], g["adev"], g["dvec_s"], g["ident_r"]
    ident_b = g["ident_b"]
    cst = g["cst"]
    cc_x1 = g["cc_x1"]
    adev_rdy = g["adev_rdy"]
    gb = g["gbuf_t"].ap()
    gbufB = [gb[:, 0:TH // 2].bitcast(BF16), gb[:, TH // 2:TH].bitcast(BF16)]

    P.scope = "p1"
    pool.push()
    dtT = [pool.take(2 * LB, BF16, name="dtT") for _ in range(NDT)]
    u_b = [pool.take(2 * LB, BF16, name="u_b") for _ in range(NDT)]
    abuf = [pool.take(TH, F32, name="abuf") for _ in range(2)]
    hlast = pool.take(NDT * NST, F32, name="hlast")
    Bb = [pool.take(TH, BF16, name="Bb") for _ in range(2)]
    Cb = [pool.take(TH, BF16, name="Cb") for _ in range(2)]
    pool.push()
    dtinT_b = pool.take(2 * LB, BF16, parts=RDT, name="dtinT_b")
    sp1 = [pool.take(512, BF16, name="sp1") for _ in range(2)]
    sp2 = [pool.take(512, BF16, name="sp2") for _ in range(2)]
    pool.pop()
    vbuf = [pool.take(TH, BF16, name="vbuf") for _ in range(2)]
    hbuf = [pool.take(TH, BF16, name="hbuf") for _ in range(2)]
    yst = pool.take(TH // 2, F32, name="yst")
    xlnT_s = u_b

    for j in range(NCORE):
        P.emit("gpsimd", lambda e, j=j: e.dma_start(out=dtinT_b[:, j * TPC:(j + 1) * TPC],
                                                    in_=ag1_out[160 * j:160 * j + 128, :]),
               wait=[(s_cc, cc_x1 - 1)] if j == 0 else [], inc=[(s_dt, 16)])
    g_dtin = P.cur(s_dt)

    pDT = [bank[4].ap(), bank[5].ap()]
    st = dict(act_sp=[0, 0], v_dt=[0, 0], v_u=0)

    for dt_i in range(NDT):
        xb = dt_i % 2
        for j in range(NCORE):
            w = []
            if dt_i == 0 and j == 0:
                w = [(s_cc, cc_x1)]
            if dt_i >= 2 and j == 0:
                w = [(s_vec, st[f"vu{dt_i - 2}"])]
            P.emit("gpsimd", lambda e, j=j, dt_i=dt_i: e.dma_start(
                    out=u_b[dt_i][:, j * TPC:(j + 1) * TPC],
                    in_=a2a1_out[j, dt_i * 128:(dt_i + 1) * 128, :]),
                   wait=w, inc=[(s_xt[xb], 16)])
        g_xln = P.cur(s_xt[xb])
        for ch in range(16):
            pb = ch % 2
            w = [(s_dt, g_dtin)]
            if st["act_sp"][pb]:
                w.append((s_act, st["act_sp"][pb]))
            P.emit("tensor", lambda e, dt_i=dt_i, ch=ch, pb=pb: e.matmul(
                    pDT[pb], dtpwT_b[:, dt_i * 128:(dt_i + 1) * 128],
                    dtinT_b[:, ch * 512:(ch + 1) * 512], start=True, stop=True),
                   wait=w, inc=[(s_pe, 1)])
            pe_mm = P.cur(s_pe)
            bias_ap = dtpb_s[:, dt_i:dt_i + 1]
            w = [(s_pe, pe_mm)]
            if st["v_dt"][pb]:
                w.append((s_vec, st["v_dt"][pb]))
            P.emit("scalar", lambda e, pb=pb, bias_ap=bias_ap: e.activation(out=sp1[pb], in_=pDT[pb], func=AF.Abs, bias=bias_ap),
                   wait=w, inc=[(s_act, 1)])
            P.emit("scalar", lambda e, pb=pb: e.activation(out=sp2[pb], in_=sp1[pb], func=AF.Exp, scale=cst[:, 2:3]),
                   wait=[(s_act, P.cur(s_act))], inc=[(s_act, 1)])
            P.emit("scalar", lambda e, pb=pb: e.activation(out=sp1[pb], in_=sp2[pb], func=AF.Ln, bias=1.0),
                   wait=[(s_act, P.cur(s_act))], inc=[(s_act, 1)])
            P.emit("scalar", lambda e, pb=pb, bias_ap=bias_ap: e.activation(out=sp2[pb], in_=pDT[pb], func=AF.Relu, bias=bias_ap),
                   wait=[(s_act, P.cur(s_act))], inc=[(s_act, 1)])
            st["act_sp"][pb] = P.cur(s_act)
            P.emit("vector", lambda e, dt_i=dt_i, ch=ch, pb=pb: e.tensor_add(dtT[dt_i][:, ch * 512:(ch + 1) * 512], sp1[pb], sp2[pb]),
                   wait=[(s_act, st["act_sp"][pb])], inc=[(s_vec, 1)], tag="vadd")
            st["v_dt"][pb] = P.cur(s_vec)
        v_dtall = P.cur(s_vec)
        P.emit("vector", lambda e, dt_i=dt_i: e.tensor_mul(u_b[dt_i], dtT[dt_i], u_b[dt_i]),
               wait=[(s_vec, v_dtall), (s_xt[xb], g_xln)], inc=[(s_vec, 1)], tag="vu")
        st[f"vu{dt_i}"] = P.cur(s_vec)
        st["v_u"] = P.cur(s_vec)
    v_prep = P.cur(s_vec)
    a_prep = P.cur(s_act)

    stt = dict(a_free=[0, 0], v_use=[0, 0], pe_gfree=[0, 0], bc_free=[0, 0],
               ev_y_pk=[0, 0])
    pend = []
    evacs = []

    def make_evac(j0, dp, pe_y, st0):
        def flush():
            for idx, dt_i in enumerate((2 * dp, 2 * dp + 1)):
                pk = dt_i % 2
                for half in range(2):
                    prev = st0 + 16 * (2 * idx + half)
                    def evy(e, pk=pk, half=half):
                        e.copy(out=yst[:, 0:512], in_=bank[4 * pk + 2 * half].ap())
                        return e.copy(out=yst[:, 512:1024], in_=bank[4 * pk + 2 * half + 1].ap())
                    w = [(s_pe, pe_y)]
                    if prev > 0:
                        w.append((s_st, prev))
                    P.emit("scalar", evy, wait=w, inc=[(s_act, 1)])
                    a_ev = P.cur(s_act)
                    P.emit("sync", lambda e, dt_i=dt_i, j0=j0, half=half: e.dma_start(
                            out=a2a2_in[j0 + half, :, dt_i * 128:(dt_i + 1) * 128]
                                .rearrange("(bq p) c -> p bq c", p=128),
                            in_=yst.rearrange("p (bq c) -> p bq c", c=128)),
                           wait=[(s_act, a_ev)], inc=[(s_st, 16)])
        return flush

    def emit_g(pk, nb, n, scan_v):
        w = [(s_vec, P.cur(s_vec))]
        if stt["pe_gfree"][pk]:
            w.append((s_pe, stt["pe_gfree"][pk]))
        P.emit("vector", lambda e, pk=pk, nb=nb: e.tensor_mul(gbufB[pk], hbuf[pk], Cb[nb]),
               wait=w, inc=[(s_vec, 1)], tag="vg")
        g_g = P.cur(s_vec)
        if pk == 1:
            stt["bc_free"][nb] = g_g
        def tacc(e, n=n, pk=pk):
            last = None
            for tb in range(16):
                bk = 4 * pk + tb // 4
                col = (tb % 4) * 128
                last = e.matmul(bank[bk].ap()[:, col:col + 128],
                                gbufB[pk][:, tb * 128:(tb + 1) * 128],
                                ident_b,
                                start=(n == 0 and tb % 4 == 0),
                                stop=(n == NST - 1 and tb % 4 == 3))
            return last
        w = [(s_vec, g_g)]
        if n == 0 and stt["ev_y_pk"][pk]:
            w.append((s_st, stt["ev_y_pk"][pk]))
        P.emit("tensor", tacc, wait=w, inc=[(s_pe, 1)])
        stt["pe_gfree"][pk] = P.cur(s_pe)

    it = 0
    for b in range(2):
        for th in range(2):
            toff = b * LB + th * TH
            j0 = toff // TPC
            for dp in range(2):
                for n in range(NST):
                    nb = it % 2
                    if n == 1 and evacs:
                        evacs.pop(0)()
                    def bcb(e, jq, b, th, n, nb, row0, dst):
                        j = 4 * b + th * 2 + jq
                        src = ag1_out.ap()[160 * j + row0 + n:160 * j + row0 + n + 1, :].partition_broadcast(128)
                        return e.dma_start(out=dst[nb][:, jq * TPC:(jq + 1) * TPC],
                                           in_=src.rearrange("p o t -> p (o t)"))
                    w = [(s_vec, stt["bc_free"][nb])] if stt["bc_free"][nb] else []
                    P.emit("gpsimd", lambda e, b=b, th=th, n=n, nb=nb: bcb(e, 0, b, th, n, nb, 128, Bb), wait=w, inc=[(s_bb[nb], 16)], ser=False)
                    P.emit("gpsimd", lambda e, b=b, th=th, n=n, nb=nb: bcb(e, 1, b, th, n, nb, 128, Bb), inc=[(s_bb[nb], 16)], ser=False)
                    P.emit("gpsimd", lambda e, b=b, th=th, n=n, nb=nb: bcb(e, 0, b, th, n, nb, 144, Cb), inc=[(s_bb[nb], 16)], ser=False)
                    P.emit("gpsimd", lambda e, b=b, th=th, n=n, nb=nb: bcb(e, 1, b, th, n, nb, 144, Cb), inc=[(s_bb[nb], 16)], ser=False)
                    g_bc = P.cur(s_bb[nb])
                    for dt_i in (2 * dp, 2 * dp + 1):
                        pk = dt_i % 2
                        w = [(s_vec, adev_rdy), (s_vec, v_prep)]
                        if stt["a_free"][pk]:
                            w.append((s_vec, stt["a_free"][pk]))
                        P.emit("scalar", lambda e, dt_i=dt_i, n=n, pk=pk, toff=toff: e.activation(
                                out=abuf[pk], in_=dtT[dt_i][:, toff:toff + TH], func=AF.Exp,
                                scale=adev[:, dt_i * NST + n:dt_i * NST + n + 1]),
                               wait=w, inc=[(s_act, 1)])
                        a_rdy = P.cur(s_act)
                        P.emit("vector", lambda e, dt_i=dt_i, nb=nb, toff=toff, pk=pk: e.tensor_mul(
                                vbuf[pk], u_b[dt_i][:, toff:toff + TH], Bb[nb]),
                               wait=[(s_bb[nb], g_bc), (s_vec, P.cur(s_vec))], inc=[(s_vec, 1)], tag="vv")
                        if pend:
                            emit_g(*pend.pop(0))
                        hl = hlast[:, dt_i * NST + n:dt_i * NST + n + 1]
                        w = [(s_act, a_rdy), (s_vec, P.cur(s_vec))]
                        if th == 0:
                            P.emit("vector", lambda e, pk=pk: e.tensor_tensor_scan(
                                    hbuf[pk], abuf[pk], vbuf[pk], 0.0, ALU.mult, ALU.add),
                                   wait=w, inc=[(s_vec, 1)], tag="vscan")
                        else:
                            P.emit("vector", lambda e, pk=pk, hl=hl: e.tensor_tensor_scan(
                                    hbuf[pk], abuf[pk], vbuf[pk], hl, ALU.mult, ALU.add),
                                   wait=w, inc=[(s_vec, 1)], tag="vscan")
                        stt["a_free"][pk] = P.cur(s_vec)
                        stt["v_use"][pk] = P.cur(s_vec)
                        scan_v = P.cur(s_vec)
                        if th == 0:
                            P.emit("vector", lambda e, hl=hl, pk=pk: e.tensor_copy(out=hl, in_=hbuf[pk][:, TH - 1:TH]),
                                   wait=[(s_vec, P.cur(s_vec))], inc=[(s_vec, 1)], tag="vhl")
                        pend.append((pk, nb, n, scan_v))
                    it += 1
                while pend:
                    emit_g(*pend.pop(0))
                pe_y = P.cur(s_pe)
                st0 = P.cur(s_st)
                stt["ev_y_pk"] = [st0 + 32, st0 + 64]
                evacs.append(make_evac(j0, dp, pe_y, st0))

    while evacs:
        evacs.pop(0)()
    P.scope = "x2"
    st_yall = P.cur(s_st)
    if g.get("tap_yslab") is not None:
        P.emit("sync", lambda e: e.dma_start(out=g["tap_yslab"][:, :, :], in_=a2a2_in[:, :, :]),
               wait=[(s_st, st_yall)], inc=[(s_st, 16)])
    def a2a2(e):
        with allow_ici_all_to_all():
            return e.collective_compute(
                "AllToAll", ALU.bypass, replica_groups=[list(range(NCORE))],
                ins=[a2a2_in.ap().rearrange("g t d -> (g t) d")],
                outs=[a2a2_out.ap().rearrange("g t d -> (g t) d")])
    P.emit("gpsimd", a2a2, wait=[(s_st, st_yall)], inc=[(s_cc, 1)], ser=False)
    env["cc_x2"] = P.cur(s_cc)
    pool.pop()


def emit_phase23(nc, P, pool, bank, env):
    g = env
    s_ld, s_gld, s_vec, s_act, s_pe, s_cc, s_st, s_ev = (
        g["s_ld"], g["s_gld"], g["s_vec"], g["s_act"], g["s_pe"], g["s_cc"], g["s_st"], g["s_ev"])
    s_gc, s_ym, s_xr, s_p4, s_w = g["s_gc"], g["s_ym"], g["s_xr"], g["s_p4"], g["s_w"]
    dvrow = g["dvrow"]
    x_own, out2_h = g["x_own"], g["out2_h"]
    a2a2_out, ar3_in, ar3_out, rt_bounce = g["a2a2_out"], g["ar3_in"], g["ar3_out"], g["rt_bounce"]
    lg_bounce = g["lg_bounce"]
    out1_scr, ffn_scr = g["out1_scr"], g["ffn_scr"]
    w_T, ffnb_s, aallT, bstkT, rwT, rbias_s = g["w_T"], g["ffnb_s"], g["aallT"], g["bstkT"], g["rwT"], g["rbias_s"]
    ident_b, ident_r, qmask_b = g["ident_b"], g["ident_r"], g["qmask_b"]
    cst = g["cst"]
    cc_x2 = env["cc_x2"]
    tap_ym, tap_out1, tap_xln3, tap_rt, tap_t1s = g["tap_ym"], g["tap_out1"], g["tap_xln3"], g["tap_rt"], g["tap_t1s"]
    tap_lgx, tap_qsf, tap_qsb = g["tap_lgx"], g["tap_qsf"], g["tap_qsb"]

    P.scope = "p2"
    pool.push()
    Db = pool.take(DM, BF16, name="Db")
    xtmp = pool.take(DM // 2, BF16, name="xtmp")
    ymb = [pool.take(DM, F32, name="ymb") for _ in range(2)]
    xrb = [pool.take(DM, F32, name="xrb") for _ in range(2)]
    st48 = pool.take(48, F32, name="st48b")
    mv2 = pool.take(2, F32, name="mv2")
    rstd2 = pool.take(2, F32, name="rstd2")
    o1b = pool.take(DM, F32, name="o1b")
    xln3T = pool.take(32 * TPC, BF16, name="xln3T")
    qsb = pool.take(32, BF16, name="qsb")
    qsf = pool.take(32, F32, name="qsf")
    lgt = pool.take(1, F32, parts=8, name="lgt")
    lrow = pool.take(8, F32, parts=1, name="lrow")
    smax = pool.take(4, F32, parts=1, name="smax")
    rt128 = pool.take(1, F32, name="rt128")
    t1s = pool.take(TPC, BF16, name="t1s")
    aall_b = pool.take(32 * 128, BF16, name="aall_b")
    bstk_b = pool.take(32 * 128, BF16, name="bstk_b")
    rwT_b = pool.take(32 * 8, BF16, name="rwT_b")
    pool.push()
    xln3_b = [pool.take(DM, BF16, name="xln3_b") for _ in range(2)]
    pool.pop()
    wslab = [pool.take(32 * 128, BF16, name="wslab") for _ in range(2)]
    fsb = g["fsb_t"].ap()
    fstg = ymb[0][:, 0:TPC]
    f4 = ymb
    o4 = xrb
    ob = o1b

    P.emit("gpsimd", lambda e: e.dma_start(out=Db,
            in_=dvrow.partition_broadcast(128).rearrange("p o t -> p (o t)")), inc=[(s_gc, 16)])
    P.emit("gpsimd", lambda e: e.dma_start(out=aall_b.rearrange("p (c m) -> p c m", m=128), in_=aallT.rearrange("(c p) m -> p c m", p=128)), inc=[(s_gc, 16)])
    P.emit("gpsimd", lambda e: e.dma_start(out=bstk_b, in_=bstkT[:, :]), inc=[(s_gc, 16)])
    P.emit("gpsimd", lambda e: e.dma_start(out=rwT_b.rearrange("p (c m) -> p c m", m=8), in_=rwT.rearrange("(c p) m -> p c m", p=128)), inc=[(s_gc, 16)])
    g_wts = P.cur(s_gc)

    pTq = bank[4].ap()[:, 0:32]
    pT3 = [bank[6].ap().bitcast(BF16), bank[7].ap().bitcast(BF16)]

    v_free = [0, 0]
    pe_x3 = [0, 0]
    for tt in range(8):
        bi = tt % 2
        for gg in range(NCORE):
            w = []
            if tt == 0 and gg == 0:
                w = [(s_cc, cc_x2)]
            if tt >= 2 and gg == 0:
                w = [(s_vec, v_free[bi])]
            P.emit("gpsimd", lambda e, tt=tt, gg=gg, bi=bi: e.dma_start(
                    out=ymb[bi][:, gg * DSH:(gg + 1) * DSH],
                    in_=a2a2_out[gg, tt * 128:(tt + 1) * 128, :]),
                   wait=w, inc=[(s_ym[bi], 16)])
        g_ym = P.cur(s_ym[bi])
        P.emit("sync", lambda e, tt=tt, bi=bi: e.dma_start(out=xrb[bi], in_=x_own[tt * 128:(tt + 1) * 128, :]),
               wait=[(s_vec, v_free[bi])] if tt >= 2 else [], inc=[(s_xr[bi], 16)])
        ld_x = P.cur(s_xr[bi])
        if tap_ym is not None:
            P.emit("sync", lambda e, tt=tt, bi=bi: e.dma_start(out=tap_ym[tt * 128:(tt + 1) * 128, :], in_=ymb[bi]),
                   wait=[(s_ym[bi], g_ym)], inc=[(s_st, 16)])

        def statsx(e, bi=bi):
            last = None
            for c in range(8):
                last = e.bn_stats(st48[:, c * 6:(c + 1) * 6], xrb[bi][:, c * 512:(c + 1) * 512])
            return last
        P.emit("vector", statsx, wait=[(s_xr[bi], ld_x)], inc=[(s_vec, 1)], tag="v2")
        P.emit("vector", lambda e: e.bn_aggr(mv2, st48), wait=[(s_vec, P.cur(s_vec))], inc=[(s_vec, 1)], tag="v2")
        P.emit("scalar", lambda e: e.activation(out=rstd2[:, 0:1], in_=mv2[:, 1:2], func=AF.Ln, bias=cst[:, 0:1]),
               wait=[(s_vec, P.cur(s_vec))], inc=[(s_act, 1)])
        P.emit("scalar", lambda e: e.activation(out=rstd2[:, 1:2], in_=rstd2[:, 0:1], func=AF.Exp, scale=cst[:, 1:2]),
               wait=[(s_act, P.cur(s_act))], inc=[(s_act, 1)])
        for hh in range(2):
            hs = slice(hh * (DM // 2), (hh + 1) * (DM // 2))
            P.emit("vector", lambda e, bi=bi, hs=hs: e.tensor_scalar(xtmp, xrb[bi][:, hs], mv2[:, 0:1], rstd2[:, 1:2], ALU.subtract, ALU.mult),
                   wait=[(s_act, P.cur(s_act)), (s_vec, P.cur(s_vec))], inc=[(s_vec, 1)], tag="v2")
            P.emit("vector", lambda e, hs=hs: e.tensor_mul(xtmp, xtmp, Db[:, hs]),
                   wait=[(s_vec, P.cur(s_vec)), (s_gc, g_wts)], inc=[(s_vec, 1)], tag="v2")
            P.emit("vector", lambda e, bi=bi, hs=hs: e.tensor_add(ymb[bi][:, hs], ymb[bi][:, hs], xtmp),
                   wait=[(s_vec, P.cur(s_vec)), (s_ym[bi], g_ym)], inc=[(s_vec, 1)], tag="v2")
        def stats2(e, bi=bi):
            last = None
            for c in range(8):
                last = e.bn_stats(st48[:, c * 6:(c + 1) * 6], ymb[bi][:, c * 512:(c + 1) * 512])
            return last
        P.emit("vector", stats2, wait=[(s_vec, P.cur(s_vec))], inc=[(s_vec, 1)], tag="v2")
        P.emit("vector", lambda e: e.bn_aggr(mv2, st48), wait=[(s_vec, P.cur(s_vec))], inc=[(s_vec, 1)], tag="v2")
        P.emit("scalar", lambda e: e.activation(out=rstd2[:, 0:1], in_=mv2[:, 1:2], func=AF.Ln, bias=cst[:, 0:1]),
               wait=[(s_vec, P.cur(s_vec))], inc=[(s_act, 1)])
        P.emit("scalar", lambda e: e.activation(out=rstd2[:, 1:2], in_=rstd2[:, 0:1], func=AF.Exp, scale=cst[:, 1:2]),
               wait=[(s_act, P.cur(s_act))], inc=[(s_act, 1)])
        a_r2 = P.cur(s_act)
        P.emit("vector", lambda e, bi=bi: e.tensor_scalar(o1b, ymb[bi], mv2[:, 0:1], rstd2[:, 1:2], ALU.subtract, ALU.mult),
               wait=[(s_act, a_r2), (s_xr[bi], ld_x)] + ([(s_st, st_o1)] if tt >= 1 else []), inc=[(s_vec, 1)], tag="v2")
        P.emit("vector", lambda e, bi=bi: e.tensor_add(o1b, o1b, xrb[bi]), wait=[(s_vec, P.cur(s_vec))], inc=[(s_vec, 1)], tag="v2")
        v_o1 = P.cur(s_vec)
        v_free[bi] = v_o1
        P.emit("sync", lambda e, tt=tt: e.dma_start(out=out1_scr[tt * 128:(tt + 1) * 128, :], in_=o1b),
               wait=[(s_vec, v_o1)], inc=[(s_st, 16)])
        st_o1 = P.cur(s_st)
        if tap_out1 is not None:
            P.emit("sync", lambda e, tt=tt: e.dma_start(out=tap_out1[tt * 128:(tt + 1) * 128, :], in_=o1b),
                   inc=[(s_st, 16)])
        def stats3(e):
            last = None
            for c in range(8):
                last = e.bn_stats(st48[:, c * 6:(c + 1) * 6], o1b[:, c * 512:(c + 1) * 512])
            return last
        P.emit("vector", stats3, wait=[(s_vec, v_o1)], inc=[(s_vec, 1)], tag="v2")
        P.emit("vector", lambda e: e.bn_aggr(mv2, st48), wait=[(s_vec, P.cur(s_vec))], inc=[(s_vec, 1)], tag="v2")
        P.emit("scalar", lambda e: e.activation(out=rstd2[:, 0:1], in_=mv2[:, 1:2], func=AF.Ln, bias=cst[:, 0:1]),
               wait=[(s_vec, P.cur(s_vec))], inc=[(s_act, 1)])
        P.emit("scalar", lambda e: e.activation(out=rstd2[:, 1:2], in_=rstd2[:, 0:1], func=AF.Exp, scale=cst[:, 1:2]),
               wait=[(s_act, P.cur(s_act))], inc=[(s_act, 1)])
        w = [(s_act, P.cur(s_act))]
        if tt >= 2:
            w.append((s_pe, pe_x3[bi]))
        P.emit("vector", lambda e, bi=bi: e.tensor_scalar(xln3_b[bi], o1b, mv2[:, 0:1], rstd2[:, 1:2], ALU.subtract, ALU.mult),
               wait=w, inc=[(s_vec, 1)], tag="v2")
        v_x3 = P.cur(s_vec)
        if tap_xln3 is not None:
            P.emit("gpsimd", lambda e, tt=tt, bi=bi: e.dma_start(out=tap_xln3[tt * 128:(tt + 1) * 128, :], in_=xln3_b[bi]),
                   wait=[(s_vec, v_x3)], inc=[(s_gld, 16)])
        def qmm(e, bi=bi, tt=tt):
            last = None
            for c in range(32):
                last = e.matmul(pTq[:, c:c + 1], xln3_b[bi][:, c * 128:(c + 1) * 128],
                                qmask_b[:, tt:tt + 1], start=(tt == 0 and c == 0), stop=(tt == 7 and c == 31))
            return last
        P.emit("tensor", qmm, wait=[(s_vec, v_x3)], inc=[(s_pe, 1)])
        for g4 in range(8):
            pbi = g4 % 2
            def transp3(e, bi=bi, g4=g4, pbi=pbi):
                last = None
                for c4 in range(4):
                    c = g4 * 4 + c4
                    last = e.matmul(pT3[pbi][:, c4 * 128:(c4 + 1) * 128],
                                    xln3_b[bi][:, c * 128:(c + 1) * 128],
                                    ident_b, is_transpose=True, start=True, stop=True)
                return last
            w = [(s_vec, v_x3)]
            if tt > 0 or g4 >= 2:
                w.append((s_act, a_e3[pbi]))
            P.emit("tensor", transp3, wait=w, inc=[(s_pe, 1)])
            if g4 == 7:
                pe_x3[bi] = P.cur(s_pe)
            pe_t = P.cur(s_pe)
            if tt == 0 and g4 == 0:
                a_e3 = [0, 0]
            def ev3(e, tt=tt, g4=g4, pbi=pbi):
                last = None
                for c4 in range(4):
                    c = g4 * 4 + c4
                    last = e.copy(out=xln3T.rearrange("p (c t) -> p c t", c=32)[:, c, tt * 128:(tt + 1) * 128],
                                  in_=pT3[pbi][:, c4 * 128:(c4 + 1) * 128])
                return last
            P.emit("scalar", ev3, wait=[(s_pe, pe_t)], inc=[(s_act, 1)])
            a_e3[pbi] = P.cur(s_act)
    a_x3all = P.cur(s_act)
    pe_all = P.cur(s_pe)
    if MAX_PHASE < 3:
        pool.pop()
        return
    P.emit("scalar", lambda e: e.copy(out=qsf, in_=pTq), wait=[(s_pe, pe_all)], inc=[(s_act, 1)])
    a_qe = P.cur(s_act)
    if tap_qsf is not None:
        P.emit("sync", lambda e: e.dma_start(out=tap_qsf[:, :], in_=qsf), wait=[(s_act, a_qe)], inc=[(s_st, 16)])
    P.emit("sync", lambda e: e.dma_start(out=ar3_in[:, :], in_=qsf),
           wait=[(s_act, a_qe)], inc=[(s_st, 16)])
    st_q = P.cur(s_st)
    P.emit("gpsimd", lambda e: e.collective_compute(
            "AllReduce", ALU.add, replica_groups=[[0, 1, 2, 3], [4, 5, 6, 7]],
            ins=[ar3_in[:, :]], outs=[ar3_out[:, :]]),
           wait=[(s_st, st_q)], inc=[(s_cc, 1)], ser=False)
    cc_q = P.cur(s_cc)
    P.emit("gpsimd", lambda e: e.dma_start(out=qsb, in_=ar3_out[:, :]), wait=[(s_cc, cc_q)], inc=[(s_gld, 16)])
    g_q = P.cur(s_gld)
    if tap_qsb is not None:
        P.emit("gpsimd", lambda e: e.dma_start(out=tap_qsb[:, :], in_=qsb), wait=[(s_gld, g_q)], inc=[(s_gld, 16)])

    def lmm(e):
        last = None
        for c in range(32):
            last = e.matmul(bank[5].ap()[0:8, 0:1], rwT_b.rearrange("p (c m) -> p c m", c=32)[:, c, :],
                            qsb[:, c:c + 1], start=(c == 0), stop=(c == 31))
        return last
    P.emit("tensor", lmm, wait=[(s_gld, g_q), (s_gc, g_wts)], inc=[(s_pe, 1)])
    P.emit("scalar", lambda e: e.activation(out=lgt, in_=bank[5].ap()[0:8, 0:1], func=AF.Identity, bias=rbias_s),
           wait=[(s_pe, P.cur(s_pe))], inc=[(s_act, 1)])
    a_l = P.cur(s_act)
    P.emit("sync", lambda e: e.dma_start(out=lg_bounce[:, :], in_=lgt), wait=[(s_act, a_l)], inc=[(s_st, 16)])
    st_lg = P.cur(s_st)
    P.emit("sync", lambda e: e.dma_start(out=lrow, in_=lg_bounce.ap().rearrange("e o -> o e")), wait=[(s_st, st_lg)], inc=[(s_ld, 16)])
    a_lr = P.cur(s_ld)
    if tap_lgx is not None:
        P.emit("sync", lambda e: e.dma_start(out=tap_lgx[:, :], in_=lrow), wait=[(s_ld, a_lr)], inc=[(s_st, 16)])
    P.emit("vector", lambda e: e.reduce_max(smax[:, 0:1], lrow, axis=mybir.AxisListType.X), wait=[(s_ld, a_lr)], inc=[(s_vec, 1)])
    P.emit("vector", lambda e: e.tensor_scalar_mul(smax[:, 1:2], smax[:, 0:1], -1.0), wait=[(s_vec, P.cur(s_vec))], inc=[(s_vec, 1)])
    P.emit("scalar", lambda e: e.activation(out=lrow, in_=lrow, func=AF.Exp, bias=smax[:, 1:2]),
           wait=[(s_vec, P.cur(s_vec))], inc=[(s_act, 1)])
    P.emit("vector", lambda e: e.reduce_sum(smax[:, 2:3], lrow, axis=mybir.AxisListType.X),
           wait=[(s_act, P.cur(s_act))], inc=[(s_vec, 1)])
    P.emit("vector", lambda e: e.reciprocal(smax[:, 3:4], smax[:, 2:3]), wait=[(s_vec, P.cur(s_vec))], inc=[(s_vec, 1)])
    P.emit("vector", lambda e: e.tensor_scalar(lrow, lrow, smax[:, 3:4], 2.0, ALU.mult, ALU.mult), wait=[(s_vec, P.cur(s_vec))], inc=[(s_vec, 1)])
    v_rt = P.cur(s_vec)
    if tap_rt is not None:
        P.emit("sync", lambda e: e.dma_start(out=tap_rt[:, :], in_=lrow), wait=[(s_vec, v_rt)], inc=[(s_st, 16)])
    P.emit("sync", lambda e: e.dma_start(out=rt_bounce[:, :], in_=lrow), wait=[(s_vec, v_rt)], inc=[(s_st, 16)])
    st_rt = P.cur(s_st)
    P.emit("gpsimd", lambda e: e.dma_start(out=rt128,
            in_=bass.AP(rt_bounce, 0, [[1, 8], [0, 16]]),
            allow_slow_non_contiguous=True),
           wait=[(s_st, st_rt)], inc=[(s_gld, 16)])
    g_rt = P.cur(s_gld)
    if MAX_PHASE < 4:
        pool.pop()
        return

    def t1mm(e):
        a3 = aall_b.rearrange("p (c m) -> p c m", c=32)
        x3 = xln3T.rearrange("p (c t) -> p c t", c=32)
        last = None
        for c in range(32):
            e.matmul(bank[0].ap(), a3[:, c, :], x3[:, c, 0:512], start=(c == 0), stop=(c == 31))
            last = e.matmul(bank[1].ap(), a3[:, c, :], x3[:, c, 512:1024], start=(c == 0), stop=(c == 31))
        return last
    P.emit("tensor", t1mm, wait=[(s_act, a_x3all), (s_gc, g_wts)], inc=[(s_pe, 1)])
    pe_t1 = P.cur(s_pe)
    P.emit("vector", lambda e: (e.tensor_scalar(t1s[:, 0:512], bank[0].ap(), rt128, None, ALU.mult),
                                e.tensor_scalar(t1s[:, 512:1024], bank[1].ap(), rt128, None, ALU.mult))[-1],
           wait=[(s_pe, pe_t1), (s_gld, g_rt)], inc=[(s_vec, 1)])
    v_t1 = P.cur(s_vec)
    if tap_t1s is not None:
        P.emit("sync", lambda e: e.dma_start(out=tap_t1s[:, :], in_=t1s), wait=[(s_vec, v_t1)], inc=[(s_st, 16)])

    a_fs = 0
    st_f = 0
    pe_fb = [0, 0]
    a_wfree = [0, 0]
    for oc in range(32):
        wb = oc % 2
        w = [(s_pe, pe_fb[wb])] if pe_fb[wb] else [(s_pe, pe_all), (s_act, a_x3all)]
        P.emit("gpsimd", lambda e, oc=oc, wb=wb: e.dma_start(
                out=wslab[wb].rearrange("p (c o) -> p c o", o=128), in_=w_T[:, oc * 128:(oc + 1) * 128].rearrange("(c p) o -> p c o", p=128)),
               wait=w, inc=[(s_w[wb], 16)])
        g_w = P.cur(s_w[wb])
        pb0 = 2 if (oc % 2 == 0) else 4
        def ffnmm(e, oc=oc, wb=wb, pb0=pb0):
            ws = wslab[wb].rearrange("p (c o) -> p c o", c=32)
            x3 = xln3T.rearrange("p (c t) -> p c t", c=32)
            for c in range(32):
                e.matmul(bank[pb0].ap(), ws[:, c, :], x3[:, c, 0:512], start=(c == 0), stop=False)
                e.matmul(bank[pb0 + 1].ap(), ws[:, c, :], x3[:, c, 512:1024], start=(c == 0), stop=False)
            bs = bstk_b.rearrange("p (c o) -> p c o", c=32)
            e.matmul(bank[pb0].ap(), bs[:, oc, :], t1s[:, 0:512], start=False, stop=True)
            return e.matmul(bank[pb0 + 1].ap(), bs[:, oc, :], t1s[:, 512:1024], start=False, stop=True)
        w = [(s_w[wb], g_w), (s_vec, v_t1)]
        if oc == 1:
            w.append((s_act, a_qe))
        if a_fs and oc >= 2:
            w.append((s_act, a_fsb[wb]))
        P.emit("tensor", ffnmm, wait=w, inc=[(s_pe, 1)])
        pe_fb[wb] = P.cur(s_pe)
        if oc == 0:
            a_fsb = [0, 0]
        def evf(e, oc=oc, pb0=pb0):
            e.activation(out=fsb[:, 0:512], in_=bank[pb0].ap(), func=AF.Identity, bias=ffnb_s[:, oc:oc + 1])
            return e.activation(out=fsb[:, 512:1024], in_=bank[pb0 + 1].ap(), func=AF.Identity, bias=ffnb_s[:, oc:oc + 1])
        P.emit("scalar", evf, wait=[(s_pe, pe_fb[wb])] + ([(s_pe, pe_tb)] if oc > 0 else []), inc=[(s_act, 1)])
        a_fs = P.cur(s_act)
        a_fsb[wb] = a_fs
        def tback(e, oc=oc):
            last = None
            for tb in range(8):
                bk = 6 + tb // 4
                col = (tb % 4) * 128
                last = e.matmul(bank[bk].ap().bitcast(F32R)[:, col:col + 128],
                                fsb[:, tb * 128:(tb + 1) * 128],
                                ident_r, is_transpose=True, start=True, stop=True)
            return last
        w = [(s_act, a_fs)]
        if oc > 0:
            w.append((s_act, a_ftg))
        P.emit("tensor", tback, wait=w, inc=[(s_pe, 1)])
        pe_tb = P.cur(s_pe)
        P.emit("scalar", lambda e: (e.copy(out=fstg[:, 0:512], in_=bank[6].ap()),
                                    e.copy(out=fstg[:, 512:1024], in_=bank[7].ap()))[-1],
               wait=[(s_pe, pe_tb)] + ([(s_st, st_f)] if st_f else []), inc=[(s_act, 1)])
        a_ftg = P.cur(s_act)
        P.emit("sync", lambda e, oc=oc: e.dma_start(
                out=ffn_scr.ap()[:, oc * 128:(oc + 1) * 128].rearrange("(q p) c -> p q c", p=128),
                in_=fstg.rearrange("p (q c) -> p q c", c=128)),
               wait=[(s_act, a_ftg)], inc=[(s_st, 16)])
        st_f = P.cur(s_st)

    if MAX_PHASE < 5:
        pool.pop()
        return
    P.scope = "p4"
    st_out = 0
    for tt in range(8):
        bi = tt % 2
        w = [(s_st, st_f)]
        if tt >= 2:
            w = [(s_vec, v_out[bi]), (s_st, st_ob_bi[bi])]
        P.emit("sync", lambda e, tt=tt, bi=bi: e.dma_start(out=f4[bi], in_=ffn_scr[tt * 128:(tt + 1) * 128, :]),
               wait=w, inc=[(s_p4[bi], 16)])
        P.emit("sync", lambda e, tt=tt, bi=bi: e.dma_start(out=o4[bi], in_=out1_scr[tt * 128:(tt + 1) * 128, :]),
               inc=[(s_p4[bi], 16)])
        ld_f = P.cur(s_p4[bi])
        def stats4(e, bi=bi):
            last = None
            for c in range(8):
                last = e.bn_stats(st48[:, c * 6:(c + 1) * 6], f4[bi][:, c * 512:(c + 1) * 512])
            return last
        P.emit("vector", stats4, wait=[(s_p4[bi], ld_f)], inc=[(s_vec, 1)])
        P.emit("vector", lambda e: e.bn_aggr(mv2, st48), wait=[(s_vec, P.cur(s_vec))], inc=[(s_vec, 1)], tag="v2")
        P.emit("scalar", lambda e: e.activation(out=rstd2[:, 0:1], in_=mv2[:, 1:2], func=AF.Ln, bias=cst[:, 0:1]),
               wait=[(s_vec, P.cur(s_vec))], inc=[(s_act, 1)])
        P.emit("scalar", lambda e: e.activation(out=rstd2[:, 1:2], in_=rstd2[:, 0:1], func=AF.Exp, scale=cst[:, 1:2]),
               wait=[(s_act, P.cur(s_act))], inc=[(s_act, 1)])
        P.emit("vector", lambda e, bi=bi: e.tensor_scalar(f4[bi], f4[bi], mv2[:, 0:1], rstd2[:, 1:2], ALU.subtract, ALU.mult),
               wait=[(s_act, P.cur(s_act))], inc=[(s_vec, 1)])
        P.emit("vector", lambda e, bi=bi: e.tensor_add(f4[bi], f4[bi], o4[bi]), wait=[(s_vec, P.cur(s_vec))], inc=[(s_vec, 1)])
        v_out = v_out if tt > 0 else [0, 0]
        st_ob_bi = st_ob_bi if tt > 0 else [0, 0]
        v_out[bi] = P.cur(s_vec)
        P.emit("sync", lambda e, tt=tt, bi=bi: e.dma_start(out=out2_h[tt * 128:(tt + 1) * 128, :], in_=f4[bi]),
               wait=[(s_vec, v_out[bi])], inc=[(s_st, 16)])
        st_ob_bi[bi] = P.cur(s_st)
        st_out = P.cur(s_st)
    pool.pop()


def _prep_inputs(inputs):
    x = np.ascontiguousarray(inputs["L0_attn_x"], dtype=np.float32)
    A_log = np.ascontiguousarray(inputs["L0_attn_A_log"], dtype=np.float32)
    Dv = np.ascontiguousarray(inputs["L0_attn_D"], dtype=np.float32)
    xpw = np.ascontiguousarray(inputs["L0_attn_x_proj_w"], dtype=np.float32)
    dtpw = np.ascontiguousarray(inputs["L0_attn_dt_proj_w"], dtype=np.float32)
    dtpbv = np.ascontiguousarray(inputs["L0_attn_dt_proj_b"], dtype=np.float32)
    W = np.ascontiguousarray(inputs["L0_ffn_W"], dtype=np.float32)
    fb = np.ascontiguousarray(inputs["L0_ffn_b"], dtype=np.float32)
    Ae = np.ascontiguousarray(inputs["L0_ffn_A"], dtype=np.float32)
    Be = np.ascontiguousarray(inputs["L0_ffn_B"], dtype=np.float32)
    rW = np.ascontiguousarray(inputs["L0_ffn_router_W"], dtype=np.float32)
    rb = np.ascontiguousarray(inputs["L0_ffn_router_b"], dtype=np.float32)

    xpwT_h = np.ascontiguousarray(xpw.T)
    w_T_h = np.ascontiguousarray(W.T)
    aallT_h = np.ascontiguousarray(Ae.transpose(2, 0, 1).reshape(128 * NST // 16, -1).T
                                   if False else Ae.transpose(2, 0, 1).reshape(-1, 128))
    aallT_h = np.ascontiguousarray(Ae.transpose(2, 0, 1).reshape(4096, 128))
    bstkT_h = np.ascontiguousarray(Be.transpose(0, 2, 1).reshape(128, 4096))
    rwT_h = np.ascontiguousarray(rW.T)
    ident = np.eye(128, dtype=np.float32)
    ffnb_h = np.ascontiguousarray(fb.reshape(32, 128).T)

    in_maps = []
    for k in range(NCORE):
        beta = k // 4
        t0 = (k % 4) * TPC
        sh = slice(DSH * k, DSH * (k + 1))
        dtpwT_h = np.ascontiguousarray(dtpw[sh, :].T)
        dtpb_h = np.ascontiguousarray(dtpbv[sh].reshape(NDT, 128).T)
        alog_h = np.ascontiguousarray(A_log[sh].reshape(NDT, 128, NST).transpose(1, 0, 2).reshape(128, NDT * NST))
        dvec_h = np.ascontiguousarray(Dv[sh].reshape(NDT, 128).T)
        gidx = np.arange(t0, t0 + TPC)
        qm = ((gidx >= Q0) & (gidx < Q1)).astype(np.float32) / float(Q1 - Q0)
        in_maps.append({
            "x_own": np.ascontiguousarray(x[beta, t0:t0 + TPC, :]),
            "xpwT": xpwT_h, "dtpwT": dtpwT_h, "dtpb": dtpb_h, "alog": alog_h,
            "dvecs": dvec_h, "w_T": w_T_h, "ffnb": ffnb_h, "aallT": aallT_h,
            "bstkT": bstkT_h, "rwT": rwT_h, "rbias": rb.reshape(8, 1),
            "identf": ident, "qmask": np.ascontiguousarray(qm.reshape(8, 128).T), "dvrow": Dv.reshape(1, 4096),
        })
    return in_maps


def kernel(**inputs):
    in_maps = _prep_inputs(inputs)
    nc, taps = build_program()
    res = run_bass_kernel_spmd(nc, in_maps, core_ids=list(range(NCORE)))
    out = np.empty((2, 4096, 4096), dtype=np.float32)
    for k in range(NCORE):
        beta = k // 4
        t0 = (k % 4) * TPC
        out[beta, t0:t0 + TPC, :] = res.results[k]["out2"]
    kernel._last_results = res
    return out
